# version 18
# baseline (speedup 1.0000x reference)
import contextlib
import math
import numpy as np
import ml_dtypes
import concourse.bass as bass
import concourse.mybir as mybir
from concourse.bass_utils import run_bass_kernel_spmd

F32 = mybir.dt.float32
BF16 = mybir.dt.bfloat16
AF = mybir.ActivationFunctionType
OP = mybir.AluOpType
AX = mybir.AxisListType.X

T = 4096
D = 2048
H = 16
DH = 128
NI = 16
SCALE = 1.0 / math.sqrt(128.0)
EPS = 1e-5
THIRD = 1.0 / 3.0

ENGINES = ("sync", "scalar", "vector", "tensor", "gpsimd")
DBG_SKIP = set()
DBG_LN_TILE = [0]


class Sem:
    def __init__(self, nc, name, stack):
        self.handle = stack.enter_context(nc.semaphore(name))
        self.n = 0


class Phase:

    def __init__(self, nc, name):
        self.nc = nc
        self.name = name
        self.ops = {e: [] for e in ENGINES}

    def em(self, e, waits, fn, inc=None, k=1):
        if inc is not None:
            inc.n += k
        self.ops[e].append((tuple(waits), fn, inc, k))

    def run(self):
        def body(eng, lst):
            for waits, fn, inc, k in lst:
                for (s, v) in waits:
                    eng.wait_ge(s.handle, v)
                inst = fn(eng)
                if inc is not None:
                    inst.then_inc(inc.handle, k)

        with self.nc.Block(self.name) as blk:
            for e in ENGINES:
                if self.ops[e]:
                    getattr(blk, e)(lambda eng, lst=self.ops[e]: body(eng, lst))


def build_nc(stop=None, taps=()):
    lndbg_state = []
    nc = bass.Bass("TRN2", target_bir_lowering=False)

    x_full = nc.dram_tensor("x_full", [T, D], F32, kind="ExternalInput").ap()
    x_own = nc.dram_tensor("x_own", [T // 2, D], F32, kind="ExternalInput").ap()
    wqT = nc.dram_tensor("wqT", [D, D], F32, kind="ExternalInput").ap()
    wkT = nc.dram_tensor("wkT", [D, D], F32, kind="ExternalInput").ap()
    wvT = nc.dram_tensor("wvT", [D, D], F32, kind="ExternalInput").ap()
    woT = nc.dram_tensor("woT", [D, D], F32, kind="ExternalInput").ap()
    wfT = nc.dram_tensor("wfT", [D, D], F32, kind="ExternalInput").ap()
    fbias = nc.dram_tensor("fbias", [1, D], F32, kind="ExternalInput").ap()
    ropeC = nc.dram_tensor("ropeC", [128, T], F32, kind="ExternalInput").ap()
    ropeS = nc.dram_tensor("ropeS", [128, T], F32, kind="ExternalInput").ap()
    ropeCq = nc.dram_tensor("ropeCq", [128, T // 2], F32, kind="ExternalInput").ap()
    ropeSq = nc.dram_tensor("ropeSq", [128, T // 2], F32, kind="ExternalInput").ap()
    maskP_in = nc.dram_tensor("maskP_in", [128, 128], BF16, kind="ExternalInput").ap()
    maskQ_in = nc.dram_tensor("maskQ_in", [128, 128], BF16, kind="ExternalInput").ap()
    ident_in = nc.dram_tensor("ident_in", [128, 128], BF16, kind="ExternalInput").ap()

    out_own = nc.dram_tensor("out_own", [T // 2, D], F32, kind="ExternalOutput").ap()
    if "lndbg" in taps:
        dbg_ln = nc.dram_tensor("dbg_ln", [128, 8], F32, kind="ExternalOutput").ap()
    if "xn1dbg" in taps:
        dbg_xn1 = nc.dram_tensor("dbg_xn1", [128, NI, T // 2], BF16, kind="ExternalOutput").ap()
    if "p6dbg" in taps:
        dbg_alpha = nc.dram_tensor("dbg_alpha", [128, D], F32, kind="ExternalOutput").ap()
        dbg_bias = nc.dram_tensor("dbg_bias", [128, D], F32, kind="ExternalOutput").ap()
        dbg_sgn = nc.dram_tensor("dbg_sgn", [128, NI, D], BF16, kind="ExternalOutput").ap()
        dbg_f = nc.dram_tensor("dbg_f", [128, D], F32, kind="ExternalOutput").ap()

    def scr_kind(n):
        return "ExternalOutput" if n in taps else "Internal"
    k_scr = nc.dram_tensor("k_scr", [H, DH, T], BF16, kind=scr_kind("k_scr")).ap()
    q_scr = nc.dram_tensor("q_scr", [H, DH, T // 2], BF16, kind=scr_kind("q_scr")).ap()
    v_scr = nc.dram_tensor("v_scr", [T, D], BF16, kind=scr_kind("v_scr")).ap()
    x1_scr = nc.dram_tensor("x1_scr", [T // 2, D], F32, kind=scr_kind("x1_scr")).ap()
    xn3_scr = nc.dram_tensor("xn3_scr", [T // 2, D], BF16, kind=scr_kind("xn3_scr")).ap()

    est = contextlib.ExitStack()

    s_dx = Sem(nc, "s_dx", est)
    s_dc = Sem(nc, "s_dc", est)
    s_dh = Sem(nc, "s_dh", est)
    s_dk = Sem(nc, "s_dk", est)
    s_do = Sem(nc, "s_do", est)
    s_dt = Sem(nc, "s_dt", est)
    s_pe = Sem(nc, "s_pe", est)
    s_act = Sem(nc, "s_act", est)
    s_dve = Sem(nc, "s_dve", est)
    s_pool = Sem(nc, "s_pool", est)
    ALL_SEMS = (s_dx, s_dc, s_dh, s_dk, s_do, s_dt, s_pe, s_act, s_dve, s_pool)

    def barrier(tag):
        p = Phase(nc, "bar_" + tag)
        snap = [(s, s.n) for s in ALL_SEMS if s.n > 0]
        for e in ENGINES:
            p.em(e, snap, lambda eng: None)
        p.run()

    def sb(name, shape, dtype):
        return est.enter_context(nc.sbuf_tensor(name, shape, dtype))

    def _fin():
        pf = Phase(nc, "fin")
        snap = [(s, s.n) for s in ALL_SEMS if s.n > 0]
        pf.em("gpsimd", snap, lambda e: e.dma_start(
            out=out_own[0:128, :128], in_=maskP_sb[:]), inc=s_do, k=16)
        pf.em("sync", [(s_do, s_do.n)], lambda e: None)
        pf.run()
        return nc

    maskP_sb = sb("maskP_sb", [128, 128], BF16)
    maskQ_sb = sb("maskQ_sb", [128, 128], BF16)
    ident_sb = sb("ident_sb", [128, 128], BF16)
    ones_col = sb("ones_col", [128, 1], BF16)
    ones_row = sb("ones_row", [1, 128], BF16)
    mk2 = sb("mk2", [1, H], F32)
    mq2 = sb("mq2", [1, H], F32)
    mhat_sb = sb("mhat_sb", [128, H], F32)
    lndbg_sb = sb("lndbg_sb", [128, 8], F32) if "lndbg" in taps else None

    p = Phase(nc, "ph0")
    for dst, src in ((maskP_sb, maskP_in), (maskQ_sb, maskQ_in),
                     (ident_sb, ident_in)):
        p.em("sync", [], lambda e, d=dst, s=src: e.dma_start(out=d[:], in_=s),
             inc=s_dc, k=16)
    for tile, val in ((ones_col, 1.0), (ones_row, 1.0), (mk2, 0.0), (mq2, 0.0)):
        p.em("gpsimd", [], lambda e, t=tile, v=val: e.memset(t[:], v), inc=s_pool)
    p.run()

    prod_stack = contextlib.ExitStack()

    def prod_sb(name, shape, dtype):
        return prod_stack.enter_context(nc.sbuf_tensor(name, shape, dtype))

    xn1T = prod_sb("xn1T", [128, NI, T // 2], BF16)
    xn1To = prod_sb("xn1To", [128, NI, T // 4], BF16)
    consts_ready = s_dc.n

    def ln1_and_v(hf, dgpair, wv_bufs):
        tbase = hf * 16
        first = dgpair[0] == 0
        p = Phase(nc, f"p1_{hf}_{dgpair[0]}")
        with contextlib.ExitStack() as ph:
            sfx = f"_{hf}_{dgpair[0]}"

            def psb(name, shape, dtype):
                return ph.enter_context(nc.sbuf_tensor(name + sfx, shape, dtype))

            def pps(name, shape, dtype=F32):
                return ph.enter_context(nc.psum_tensor(name + sfx, shape, dtype))

            wv_f = [psb(f"p1wf{i}", [128, 2, 512], F32) for i in range(2)]
            wv_bf = wv_bufs[0 if first else 1]
            vst = [[psb(f"p1vs{i}{dg}", [128, 512], BF16) for dg in range(2)]
                   for i in range(2)]
            ps_v = [[pps(f"p1pv{i}{dg}", [128, 512], F32) for dg in range(2)]
                    for i in range(2)]
            if first:
                xs = [psb(f"p1x{i}", [128, D], F32) for i in range(2)]
                st24 = [psb(f"p1st{i}", [128, 24], F32) for i in range(2)]
                mv6 = [psb(f"p1mv{i}", [128, 2], F32) for i in range(2)]
                vrow = [psb(f"p1v{i}", [128, 1], F32) for i in range(2)]
                rrow = [psb(f"p1r{i}", [128, 1], F32) for i in range(2)]
                xn1 = [psb(f"p1xn{i}", [128, D], BF16) for i in range(2)]

            wv_cast = {}

            def emit_wv(pc):
                pl = pc % 16
                dgi, q8 = pl // 8, pl % 8
                dg = dgpair[dgi] + (2 if pc >= 16 else 0)
                dst = wv_bufs[1 if pc >= 16 else 0]
                w = [(s_dc, consts_ready)] if pc == 0 else []
                if pc >= 2:
                    w.append((s_act, wv_cast[pc - 2]))
                p.em("sync", w, lambda e, dg=dg, q8=q8, pc=pc: e.dma_start(
                    out=wv_f[pc % 2][:],
                    in_=wvT.rearrange("(a p) o -> p a o", p=128)[
                        :, q8 * 2:(q8 + 1) * 2, dg * 512:(dg + 1) * 512]),
                    inc=s_dx, k=16)
                ld = s_dx.n
                p.em("scalar", [(s_dx, ld)], lambda e, dgi=dgi, q8=q8, pc=pc,
                     dst=dst: e.copy(out=dst[:].rearrange(
                         "p (a b) o -> p a b o", a=8)[
                             :, q8, :, dgi * 512:(dgi + 1) * 512],
                         in_=wv_f[pc % 2][:]), inc=s_act)
                wv_cast[pc] = s_act.n

            ntiles = 24 if first else 16
            ord_apply, ord_trd, ord_v, ord_evv, ord_vst = {}, {}, {}, {}, {}

            def emit_vmms(st):
                w = [(s_act, wv_cast[15]) if first else wv_ready]
                if first:
                    w.append((s_dt, ord_trd[st]))
                if st >= 2:
                    w.append((s_dve, ord_evv[st - 2]))

                def vmms(e, st=st):
                    last = None
                    for dgi in range(2):
                        for i in range(NI):
                            last = e.matmul(
                                ps_v[st % 2][dgi][:],
                                xn1T[:, i, st * 128:(st + 1) * 128],
                                wv_bf[:, i, dgi * 512:(dgi + 1) * 512],
                                start=(i == 0), stop=(i == NI - 1))
                    return last
                p.em("tensor", w, vmms, inc=s_pe)
                ord_v[st] = s_pe.n
                w = [(s_pe, ord_v[st])]
                if st >= 2:
                    w.append((s_dk, ord_vst[st - 2]))
                p.em("vector", w, lambda e, st=st: e.tensor_copy(
                    out=vst[st % 2][0][:], in_=ps_v[st % 2][0][:]))
                p.em("vector", [], lambda e, st=st: e.tensor_copy(
                    out=vst[st % 2][1][:], in_=ps_v[st % 2][1][:]), inc=s_dve)
                ord_evv[st] = s_dve.n
                ts = tbase + st
                for dgi in range(2):
                    dg = dgpair[dgi]
                    p.em("gpsimd",
                         [(s_dve, ord_evv[st])] if dgi == 0 else [],
                         lambda e, ts=ts, st=st, dgi=dgi, dg=dg: e.dma_start(
                             out=v_scr[ts * 128:(ts + 1) * 128,
                                       dg * 512:(dg + 1) * 512],
                             in_=vst[st % 2][dgi][:]), inc=s_dk, k=16)
                ord_vst[st] = s_dk.n

            wv_ready = (s_act, wv_bufs[2]) if not first else None
            for st in range(ntiles):
                own = st >= 16
                par = st % 2
                ts = tbase + st
                if first:
                    if st < 16:
                        emit_wv(st)
                    else:
                        emit_wv(16 + (st - 16) * 2)
                        emit_wv(17 + (st - 16) * 2)
                if first:
                    if own:
                        oj = hf * 8 + (st - 16)
                        src_ap = x_own[oj * 128:(oj + 1) * 128, :]
                    else:
                        src_ap = x_full[ts * 128:(ts + 1) * 128, :]
                    w = [] if st < 2 else [(s_dve, ord_apply[st - 2])]
                    p.em("gpsimd", w, lambda e, src_ap=src_ap, par=par: e.dma_start(
                        out=xs[par][:], in_=src_ap), inc=s_dk, k=16)
                    xld = s_dk.n
                    for c4 in range(4):
                        p.em("vector", [(s_dk, xld)] if c4 == 0 else [],
                             lambda e, par=par, c4=c4: e.bn_stats(
                                 out=st24[par][:, c4 * 6:(c4 + 1) * 6],
                                 in_=xs[par][:, c4 * 512:(c4 + 1) * 512]),
                             inc=s_dve if c4 == 3 else None)
                    bnd = s_dve.n
                    p.em("vector", [(s_dve, bnd)], lambda e, par=par:
                         e.bn_aggr(out=mv6[par][:], in_=st24[par][:]),
                         inc=s_dve)
                    agg = s_dve.n
                    p.em("gpsimd", [(s_dve, agg)], lambda e, par=par:
                         e.tensor_scalar(
                             out=vrow[par][:], in0=mv6[par][:, 1:2],
                             scalar1=1.0, scalar2=EPS, op0=OP.mult,
                             op1=OP.add), inc=s_pool)
                    stats_done = s_pool.n
                    p.em("scalar", [(s_pool, stats_done)], lambda e, par=par:
                         e.activation(out=vrow[par][:], in_=vrow[par][:],
                                      func=AF.Ln))
                    p.em("scalar", [], lambda e, par=par: e.activation(
                        out=rrow[par][:], in_=vrow[par][:], func=AF.Exp,
                        scale=-0.5), inc=s_act)
                    rstd_done = s_act.n
                    w = [(s_act, rstd_done)]
                    if st >= 2:
                        w.append((s_dt, ord_trd[st - 2]))
                    p.em("vector", w, lambda e, par=par: e.tensor_scalar(
                        out=xn1[par][:], in0=xs[par][:],
                        scalar1=mv6[par][:, 0:1],
                        scalar2=rrow[par][:], op0=OP.subtract, op1=OP.mult),
                        inc=s_dve)
                    ord_apply[st] = s_dve.n
                    dstT = xn1To if own else xn1T
                    dcol = (st - 16) * 128 if own else st * 128
                    p.em("scalar", [(s_dve, ord_apply[st])], lambda e, par=par,
                         dstT=dstT, dcol=dcol: e.dma_start_transpose(
                             out=dstT[:, 0:16, dcol:dcol + 128],
                             in_=xn1[par][:]), inc=s_dt, k=16)
                    ord_trd[st] = s_dt.n
                if not first:
                    emit_vmms(st)
                elif st >= 16:
                    emit_vmms(st - 16)
            if first:
                for j in range(8, 16):
                    emit_vmms(j)
            p.run()
            return wv_cast.get(31)

    def kq_phase(hf, which):
        is_k = which == "k"
        wsrc = wkT if is_k else wqT
        dscr = k_scr if is_k else q_scr
        mdst = mk2 if is_k else mq2
        srcT = xn1T if is_k else xn1To
        ngrp = 4 if is_k else 2
        p = Phase(nc, f"p{which}_{hf}")
        with contextlib.ExitStack() as ph:
            def psb(name, shape, dtype):
                return ph.enter_context(
                    nc.sbuf_tensor(name + which + str(hf), shape, dtype))

            def pps(name, shape, dtype=F32):
                return ph.enter_context(
                    nc.psum_tensor(name + which + str(hf), shape, dtype))

            if is_k:
                cs_tab = (psb("ropeCk_sb", [128, T], F32),
                          psb("ropeSk_sb", [128, T], F32))
                rope_srcs = ((cs_tab[0], ropeC), (cs_tab[1], ropeS))
            else:
                cs_tab = (psb("ropeCq_sb", [128, T // 2], F32),
                          psb("ropeSq_sb", [128, T // 2], F32))
                rope_srcs = ((cs_tab[0], ropeCq), (cs_tab[1], ropeSq))
            for dst, src2 in rope_srcs:
                p.em("sync", [], lambda e, d=dst, s=src2: e.dma_start(
                    out=d[:], in_=s), inc=s_dx, k=16)
            cs_ld = s_dx.n

            wf = [psb(f"wf{i}", [128, NI, 128], F32) for i in range(3)]
            wbf = [psb(f"wbf{i}", [128, NI, 128], BF16) for i in range(3)]
            kswap = [psb(f"ksw{i}", [128, 512], F32) for i in range(2)]
            t1 = psb("t1", [128, 512], F32)
            t2 = psb("t2", [128, 512], F32)
            kbf = [psb(f"kbf{i}", [128, 512], BF16) for i in range(4)]
            sq = [psb(f"sq{i}", [128, 512], BF16) for i in range(4)]
            r1 = psb("r1", [1, 1], F32)
            ps_k = [pps(f"psk{i}", [128, 512]) for i in range(4)]
            ps_k2 = [pps(f"psk2{i}", [1, 512]) for i in range(2)]

            for i in range(2):
                p.em("gpsimd", [], lambda e, i=i: e.memset(kswap[i][32:64, :], 0.0))
                p.em("gpsimd", [], lambda e, i=i: e.memset(kswap[i][64:96, :], 0.0),
                     inc=s_pool)
            swz = s_pool.n

            nli = H * ngrp
            wcast, mm_last = {}, {}
            o_dve, o_swp, o_str, o_sq, o_red, k2_ord = {}, {}, {}, {}, {}, {}

            def emit_k2(lj):
                par = lj % 4
                kpar = lj % 2
                hj = lj // ngrp
                w = [(s_pool, o_sq[lj])]
                if lj >= 2:
                    w.append((s_dve, o_red[lj - 2]))
                p.em("tensor", w, lambda e, par=par, kpar=kpar: e.matmul(
                    ps_k2[kpar][:], ones_col[:], sq[par][:],
                    start=True, stop=True), inc=s_pe)
                k2_ord[lj] = s_pe.n
                p.em("vector", [(s_pe, k2_ord[lj])], lambda e, kpar=kpar:
                     e.reduce_max(out=r1[:], in_=ps_k2[kpar][:], axis=AX),
                     inc=s_dve)
                p.em("vector", [(s_dve, s_dve.n)], lambda e, hj=hj, mdst=mdst:
                     e.tensor_tensor(out=mdst[0:1, hj:hj + 1],
                                     in0=mdst[0:1, hj:hj + 1],
                                     in1=r1[:], op=OP.max), inc=s_dve)
                o_red[lj] = s_dve.n

            for h in range(H):
                wpar = h % 3
                for piece in range(2):
                    w = []
                    if piece == 0 and h >= 3:
                        w.append((s_act, wcast[h - 3]))
                    p.em("sync", w, lambda e, h=h, wpar=wpar, piece=piece:
                         e.dma_start(
                             out=wf[wpar][:, piece * 8:(piece + 1) * 8, :],
                             in_=wsrc.rearrange("(ic p) o -> p ic o", p=128)[
                                 :, piece * 8:(piece + 1) * 8,
                                 h * 128:(h + 1) * 128]), inc=s_dx, k=16)
                wld = s_dx.n
                w = [(s_dx, wld)]
                if h >= 3:
                    w.append((s_pe, mm_last[h - 3]))
                p.em("scalar", w, lambda e, wpar=wpar: e.copy(
                    out=wbf[wpar][:], in_=wf[wpar][:]), inc=s_act)
                wcast[h] = s_act.n

                for g in range(ngrp):
                    gg = hf * ngrp + g
                    li = h * ngrp + g
                    par = li % 4
                    spar = li % 2
                    w = [(s_act, wcast[h])]
                    if li >= 4:
                        w.append((s_dve, o_dve[li - 4]))
                        w.append((s_act, o_swp[li - 4]))

                    def mms(e, wpar=wpar, par=par, g=g, srcT=srcT):
                        last = None
                        for i in range(NI):
                            last = e.matmul(
                                ps_k[par][:], wbf[wpar][:, i, :],
                                srcT[:, i, g * 512:(g + 1) * 512],
                                start=(i == 0), stop=(i == NI - 1))
                        return last
                    p.em("tensor", w, mms, inc=s_pe)
                    mmord = s_pe.n
                    if g == ngrp - 1:
                        mm_last[h] = mmord
                    if li >= 2:
                        emit_k2(li - 2)
                    w = [(s_pe, mmord)]
                    if li >= 2:
                        w.append((s_dve, o_dve[li - 2]))
                    p.em("scalar", w, lambda e, par=par, spar=spar: e.copy(
                        out=kswap[spar][0:32, :], in_=ps_k[par][64:96, :]))
                    p.em("scalar", [], lambda e, par=par, spar=spar: e.copy(
                        out=kswap[spar][64:96, :], in_=ps_k[par][0:32, :]),
                        inc=s_act)
                    o_swp[li] = s_act.n
                    cs = slice(gg * 512, (gg + 1) * 512)
                    w = [(s_pe, mmord), (s_pool, swz), (s_dx, cs_ld),
                         (s_act, o_swp[li])]
                    if li >= 4:
                        w.append((s_dk, o_str[li - 4]))
                        w.append((s_pool, o_sq[li - 4]))
                    p.em("vector", w, lambda e, par=par, cs=cs, cs_tab=cs_tab:
                         e.tensor_tensor(out=t1[:], in0=ps_k[par][:],
                                         in1=cs_tab[0][:, cs], op=OP.mult))
                    p.em("vector", [], lambda e, spar=spar, cs=cs, cs_tab=cs_tab:
                         e.tensor_tensor(out=t2[:], in0=kswap[spar][:],
                                         in1=cs_tab[1][:, cs], op=OP.mult))
                    p.em("vector", [], lambda e, par=par: e.tensor_tensor(
                        out=kbf[par][:], in0=t1[:], in1=t2[:], op=OP.add),
                        inc=s_dve)
                    o_dve[li] = s_dve.n
                    p.em("gpsimd", [(s_dve, o_dve[li])], lambda e, par=par:
                         e.tensor_tensor(out=sq[par][:], in0=kbf[par][:],
                                         in1=kbf[par][:], op=OP.mult), inc=s_pool)
                    o_sq[li] = s_pool.n
                    p.em("gpsimd", [], lambda e, h=h, gg=gg, par=par, dscr=dscr:
                         e.dma_start(out=dscr[h, :, gg * 512:(gg + 1) * 512],
                                     in_=kbf[par][:]), inc=s_dk, k=16)
                    o_str[li] = s_dk.n
            emit_k2(nli - 2)
            emit_k2(nli - 1)
            p.run()

    for hf in range(2):
        p1s = contextlib.ExitStack()
        wv_bufs = [p1s.enter_context(
            nc.sbuf_tensor(f"p1wbf{hf}{i}", [128, NI, 1024], BF16))
            for i in range(2)]
        wv_bufs.append(None)
        wv_bufs[2] = ln1_and_v(hf, (0, 1), wv_bufs)
        barrier(f"va{hf}")
        if hf == 0 and "xn1dbg" in taps:
            pdb = Phase(nc, "xn1dbg")
            pdb.em("gpsimd", [], lambda e: e.dma_start(
                out=dbg_xn1, in_=xn1T[:]), inc=s_do, k=16)
            pdb.run()
            barrier("xn1dbg")
        if stop == f"va{hf}":
            return _fin()
        ln1_and_v(hf, (2, 3), wv_bufs)
        p1s.close()
        barrier(f"vb{hf}")
        if stop == f"vb{hf}":
            return _fin()
        kq_phase(hf, "k")
        barrier(f"k{hf}")
        if stop == f"k{hf}":
            return _fin()
        kq_phase(hf, "q")
        barrier(f"q{hf}")
        if stop == f"q{hf}":
            return _fin()

    prod_stack.close()

    p = Phase(nc, "mhat")
    with contextlib.ExitStack() as ph:
        prodr = ph.enter_context(nc.sbuf_tensor("prodr", [1, H], F32))
        mneg_bf = ph.enter_context(nc.sbuf_tensor("mneg_bf", [1, H], BF16))
        ps_mh = ph.enter_context(nc.psum_tensor("ps_mh", [128, H], F32))
        p.em("vector", [], lambda e: e.tensor_tensor(
            out=prodr[:], in0=mq2[:], in1=mk2[:], op=OP.mult), inc=s_dve)
        v0 = s_dve.n
        p.em("scalar", [(s_dve, v0)], lambda e: e.activation(
            out=prodr[:], in_=prodr[:], func=AF.Ln))
        p.em("scalar", [], lambda e: e.activation(
            out=prodr[:], in_=prodr[:], func=AF.Exp, scale=0.5), inc=s_act)
        a0 = s_act.n
        p.em("vector", [(s_act, a0)], lambda e: e.tensor_scalar(
            out=mneg_bf[:], in0=prodr[:], scalar1=-SCALE, scalar2=None,
            op0=OP.mult), inc=s_dve)
        v1 = s_dve.n
        p.em("tensor", [(s_dve, v1)], lambda e: e.matmul(
            ps_mh[:], ones_row[:], mneg_bf[:], start=True, stop=True), inc=s_pe)
        t0 = s_pe.n
        p.em("scalar", [(s_pe, t0)], lambda e: e.copy(
            out=mhat_sb[:], in_=ps_mh[:]), inc=s_act)
        p.run()
    barrier("mhat")
    if stop == "mhat":
        return _fin()

    attn_stack = contextlib.ExitStack()
    y_sb = attn_stack.enter_context(nc.sbuf_tensor("y_sb", [128, H, T // 2], BF16))
    wo_bf = attn_stack.enter_context(nc.sbuf_tensor("wobf", [128, NI, D], BF16))

    p = Phase(nc, "attn")
    with contextlib.ExitStack() as ph:
        def psb(name, shape, dtype):
            return ph.enter_context(nc.sbuf_tensor(name, shape, dtype))

        def pps(name, shape, dtype=F32):
            return ph.enter_context(nc.psum_tensor(name, shape, dtype))

        ksb = [psb(f"ksb{i}", [128, T], BF16) for i in range(2)]
        qsb = [psb(f"qsb{i}", [128, T // 2], BF16) for i in range(2)]
        vsb = [psb(f"vsb{i}", [128, 32, 128], BF16) for i in range(2)]
        pT = [psb(f"pT{i}", [128, 512], BF16) for i in range(8)]
        rlf = [psb(f"rlf{i}", [1, 512], F32) for i in range(2)]
        rlb_sb = [psb(f"rlb{i}", [128, 512], F32) for i in range(2)]
        ps_s = [pps(f"pss{i}", [128, 512]) for i in range(4)]
        ps_l = [pps(f"psl{i}", [1, 512]) for i in range(2)]
        ps_y = [pps(f"psy{i}", [128, 512]) for i in range(2)]

        wo_f = [psb(f"wof{i}", [128, 1, D], F32) for i in range(2)]
        hload, head_free, wo_cast = {}, {}, {}

        def emit_wo(pc):
            w = [] if pc < 2 else [(s_act, wo_cast[pc - 2])]
            p.em("sync", w, lambda e, pc=pc: e.dma_start(
                out=wo_f[pc % 2][:],
                in_=woT.rearrange("(hc p) o -> p hc o", p=128)[
                    :, pc:pc + 1, :]), inc=s_dx, k=16)
            ld = s_dx.n
            p.em("scalar", [(s_dx, ld)], lambda e, pc=pc: e.copy(
                out=wo_bf[:, pc:pc + 1, :], in_=wo_f[pc % 2][:]),
                inc=s_act)
            wo_cast[pc] = s_act.n

        exp_ord, pv_ord, msk_ord = {}, {}, {}
        yev_ord, recip_ord, rlbev_ord = {}, {}, {}

        def head_loads(h):
            par = h % 2
            w = [(s_pe, head_free[h - 2])] if h >= 2 else []
            p.em("sync", w, lambda e, h=h, par=par: e.dma_start(
                out=ksb[par][:], in_=k_scr[h]), inc=s_dh, k=16)
            p.em("sync", [], lambda e, h=h, par=par: e.dma_start(
                out=qsb[par][:], in_=q_scr[h]), inc=s_dh, k=16)
            for piece in range(4):
                p.em("sync", [], lambda e, h=h, par=par, piece=piece: e.dma_start(
                    out=vsb[par][:, piece * 8:(piece + 1) * 8, :],
                    in_=v_scr.rearrange("(sc p) d -> p sc d", p=128)[
                        :, piece * 8:(piece + 1) * 8, h * 128:(h + 1) * 128]),
                    inc=s_dh, k=16)
            hload[h] = s_dh.n
            if 2 <= h <= 9:
                emit_wo(2 * (h - 2))
                emit_wo(2 * (h - 2) + 1)

        head_loads(0)
        head_loads(1)

        blocks = []
        gb = 0
        for h in range(H):
            for j0 in (0, 4, 8, 12):
                grp = h * 4 + j0 // 4
                B = 2 * j0 + 8
                for bb in range(B):
                    blocks.append((h, j0, grp, bb, B, gb))
                    gb += 1
        NB = len(blocks)
        LAG = 3

        def emit_S(i):
            h, j0, grp, bb, B, gb = blocks[i]
            par = h % 2
            js = max(0, bb // 2 - j0)
            sslice = slice(js * 128, 512)
            qlo = j0 * 128 + js * 128
            qhi = (j0 + 4) * 128
            spar = gb % 4
            w = [(s_dh, hload[h])]
            if gb >= 4:
                w.append((s_act, exp_ord[gb - 4]))
            p.em("tensor", w, lambda e, par=par, spar=spar, bb=bb,
                 sslice=sslice, qlo=qlo, qhi=qhi: e.matmul(
                     ps_s[spar][:, sslice],
                     ksb[par][:, bb * 128:(bb + 1) * 128],
                     qsb[par][:, qlo:qhi], start=True, stop=True), inc=s_pe)
            smm = s_pe.n
            w = [(s_pe, smm)]
            if gb >= 8:
                w.append((s_pe, pv_ord[gb - 8]))
            p.em("scalar", w, lambda e, h=h, spar=spar, gb=gb,
                 sslice=sslice: e.activation(
                     out=pT[gb % 8][:, sslice], in_=ps_s[spar][:, sslice],
                     func=AF.Exp, bias=mhat_sb[:, h:h + 1], scale=SCALE),
                 inc=s_act)
            exp_ord[gb] = s_act.n
            diag = bb >= 2 * j0
            if diag:
                mtile = maskP_sb if bb % 2 == 0 else maskQ_sb
                p.em("gpsimd", [(s_act, exp_ord[gb])],
                     lambda e, gb=gb, js=js, mtile=mtile: e.tensor_tensor(
                         out=pT[gb % 8][:, js * 128:(js + 1) * 128],
                         in0=pT[gb % 8][:, js * 128:(js + 1) * 128],
                         in1=mtile[:], op=OP.mult), inc=s_pool)
                msk_ord[gb] = s_pool.n

        def emit_lpv(i):
            h, j0, grp, bb, B, gb = blocks[i]
            par = h % 2
            gpar = grp % 2
            js = max(0, bb // 2 - j0)
            sslice = slice(js * 128, 512)
            stop = (bb == B - 1)
            start = (bb == 0)
            diag = bb >= 2 * j0
            w = [(s_act, exp_ord[gb])]
            if diag:
                w.append((s_pool, msk_ord[gb]))
            if start and grp >= 2:
                w.append((s_dve, recip_ord[grp - 2]))
            p.em("tensor", w, lambda e, gpar=gpar, gb=gb, start=start,
                 stop=stop, sslice=sslice: e.matmul(
                     ps_l[gpar][:, sslice], ones_col[:],
                     pT[gb % 8][:, sslice], start=start, stop=stop))
            w = []
            if start and grp >= 2:
                w.append((s_dve, yev_ord[grp - 2]))
            p.em("tensor", w, lambda e, par=par, gpar=gpar, gb=gb, bb=bb,
                 start=start, stop=stop, sslice=sslice: e.matmul(
                     ps_y[gpar][:, sslice], vsb[par][:, bb, :],
                     pT[gb % 8][:, sslice], start=start, stop=stop), inc=s_pe)
            pv_ord[gb] = s_pe.n
            if stop:
                epi_a(h, j0, grp, gb)

        pend = {}

        def epi_a(h, j0, grp, gb):
            gpar = grp % 2
            p.em("vector", [(s_pe, pv_ord[gb])], lambda e, gpar=gpar:
                 e.reciprocal(out=rlf[gpar][:], in_=ps_l[gpar][:]), inc=s_dve)
            recip_ord[grp] = s_dve.n
            for q in (0, 32, 64, 96):
                p.em("vector", [(s_dve, recip_ord[grp])],
                     lambda e, gpar=gpar, q=q: e.tensor_copy(
                         out=rlb_sb[gpar][q:q + 1, :], in_=rlf[gpar][:]),
                     inc=s_dve if q == 96 else None)
            seed_ord = s_dve.n
            p.em("vector", [(s_dve, seed_ord)], lambda e, gpar=gpar:
                 e.stream_shuffle(out=rlb_sb[gpar][:], in_=rlb_sb[gpar][:],
                                  mask=[0] * 32), inc=s_dve)
            rlbev_ord[grp] = s_dve.n
            p.em("vector", [(s_dve, rlbev_ord[grp])], lambda e, h=h, j0=j0,
                 gpar=gpar: e.scalar_tensor_tensor(
                     out=y_sb[:, h, j0 * 128:(j0 + 4) * 128], in0=ps_y[gpar][:],
                     scalar=THIRD, in1=rlb_sb[gpar][:],
                     op0=OP.mult, op1=OP.mult), inc=s_dve)
            yev_ord[grp] = s_dve.n
            head_free[h] = pv_ord[gb]
            if j0 == 12 and h + 2 < H:
                head_loads(h + 2)

        for i in range(NB + LAG):
            if i < NB:
                emit_S(i)
            if 0 <= i - LAG < NB:
                emit_lpv(i - LAG)
        p.run()
        wo_ready = wo_cast[15]
    barrier("attn")
    if stop == "attn":
        return _fin()

    p = Phase(nc, "p5")
    with contextlib.ExitStack() as ph:
        def psb(name, shape, dtype):
            return ph.enter_context(nc.sbuf_tensor(name, shape, dtype))

        def pps(name, shape, dtype=F32):
            return ph.enter_context(nc.psum_tensor(name, shape, dtype))


        xo = [psb(f"xo{i}", [128, D], F32) for i in range(2)]
        a_sb = [psb(f"asb{i}", [128, D], F32) for i in range(2)]
        x1_sb = [psb(f"x1sb{i}", [128, D], F32) for i in range(2)]
        xn3 = [psb(f"xn3_{i}", [128, D], BF16) for i in range(2)]
        st24a = psb("p5sta", [128, 24], F32)
        st24b = psb("p5stb", [128, 24], F32)
        mv6a = psb("p5mva", [128, 2], F32)
        mv6b = psb("p5mvb", [128, 2], F32)
        vrow = [psb(f"p5v{i}", [128, 1], F32) for i in range(2)]
        rrow = [psb(f"p5r{i}", [128, 1], F32) for i in range(2)]
        ps_o = pps("ps_o", [128, D])

        o_ev, o_st, o_mm, o_add, o_ap3 = {}, {}, {}, {}, {}

        def ln_bn(src, par, st24, mv6, rv, wfirst):
            for c4 in range(4):
                p.em("vector", wfirst if c4 == 0 else [],
                     lambda e, par=par, src=src, st24=st24, c4=c4: e.bn_stats(
                         out=st24[:, c4 * 6:(c4 + 1) * 6],
                         in_=src[par][:, c4 * 512:(c4 + 1) * 512]),
                     inc=s_dve if c4 == 3 else None)
            p.em("vector", [(s_dve, s_dve.n)], lambda e, st24=st24, mv6=mv6:
                 e.bn_aggr(out=mv6[:], in_=st24[:]), inc=s_dve)
            agg = s_dve.n
            p.em("gpsimd", [(s_dve, agg)], lambda e, mv6=mv6, rv=rv:
                 e.tensor_scalar(
                     out=vrow[rv][:], in0=mv6[:, 1:2], scalar1=1.0,
                     scalar2=EPS, op0=OP.mult, op1=OP.add), inc=s_pool)
            sdone = s_pool.n
            p.em("scalar", [(s_pool, sdone)], lambda e, rv=rv: e.activation(
                out=vrow[rv][:], in_=vrow[rv][:], func=AF.Ln))
            p.em("scalar", [], lambda e, rv=rv: e.activation(
                out=rrow[rv][:], in_=vrow[rv][:], func=AF.Exp, scale=-0.5),
                inc=s_act)
            return s_act.n

        for tc in range(16):
            par = tc % 2
            w = [(s_act, wo_ready)] if tc < 2 else [(s_dve, o_add[tc - 2])]
            p.em("sync", w, lambda e, tc=tc, par=par: e.dma_start(
                out=xo[par][:], in_=x_own[tc * 128:(tc + 1) * 128, :]),
                inc=s_dx, k=16)
            xld = s_dx.n
            w = [(s_act, wo_ready)]
            if tc >= 1:
                w.append((s_act, o_ev[tc - 1]))

            def omms(e, tc=tc):
                last = None
                for oc in range(4):
                    for hj in range(16):
                        last = e.matmul(ps_o[:, oc * 512:(oc + 1) * 512],
                                        y_sb[:, hj, tc * 128:(tc + 1) * 128],
                                        wo_bf[:, hj, oc * 512:(oc + 1) * 512],
                                        start=(hj == 0), stop=(hj == 15))
                return last
            p.em("tensor", w, omms, inc=s_pe)
            o_mm[tc] = s_pe.n
            p.em("scalar", [(s_pe, o_mm[tc])], lambda e, par=par: e.copy(
                out=a_sb[par][:], in_=ps_o[:]), inc=s_act)
            o_ev[tc] = s_act.n

            rd2 = ln_bn(a_sb, par, st24a, mv6a, 0, [(s_act, o_ev[tc])])
            p.em("vector", [(s_dx, xld), (s_act, rd2)], lambda e, par=par:
                 e.tensor_scalar(
                     out=a_sb[par][:], in0=a_sb[par][:], scalar1=mv6a[:, 0:1],
                     scalar2=rrow[0][:], op0=OP.subtract, op1=OP.mult))
            w = [] if tc < 2 else [(s_do, o_st[tc - 2])]
            p.em("vector", w, lambda e, par=par: e.tensor_tensor(
                out=x1_sb[par][:], in0=a_sb[par][:], in1=xo[par][:], op=OP.add),
                inc=s_dve)
            o_add[tc] = s_dve.n
            rd3 = ln_bn(x1_sb, par, st24b, mv6b, 1, [])
            p.em("vector", [(s_act, rd3)], lambda e, par=par: e.tensor_scalar(
                out=xn3[par][:], in0=x1_sb[par][:], scalar1=mv6b[:, 0:1],
                scalar2=rrow[1][:], op0=OP.subtract, op1=OP.mult), inc=s_dve)
            o_ap3[tc] = s_dve.n
            p.em("gpsimd", [(s_dve, o_ap3[tc])], lambda e, tc=tc, par=par:
                 e.dma_start(out=xn3_scr[tc * 128:(tc + 1) * 128, :],
                             in_=xn3[par][:]), inc=s_do, k=16)
            p.em("gpsimd", [], lambda e, tc=tc, par=par: e.dma_start(
                out=x1_scr[tc * 128:(tc + 1) * 128, :], in_=x1_sb[par][:]),
                inc=s_do, k=16)
            o_st[tc] = s_do.n
        p.run()
    attn_stack.close()
    barrier("p5")
    if stop == "p5":
        return _fin()

    p = Phase(nc, "p6")
    with contextlib.ExitStack() as ph:
        def psb(name, shape, dtype):
            return ph.enter_context(nc.sbuf_tensor(name, shape, dtype))

        def pps(name, shape, dtype=F32):
            return ph.enter_context(nc.psum_tensor(name, shape, dtype))

        sgn_bf = psb("sgnbf", [128, NI, D], BF16)
        alphab = psb("alphab", [128, D], F32)
        biasb = psb("biasb", [128, D], F32)
        pre_stack = contextlib.ExitStack()

        def presb(name, shape, dtype):
            return pre_stack.enter_context(nc.sbuf_tensor(name, shape, dtype))

        wf_f = [presb(f"wff{i}", [128, D], F32) for i in range(4)]
        absw = [presb(f"absw{i}", [128, D], BF16) for i in range(4)]
        arow = presb("arow", [1, D], F32)
        ahi = presb("ahi", [1, D], BF16)
        ahif = presb("ahif", [1, D], F32)
        alo = presb("alo", [1, D], BF16)
        brow = presb("brow", [1, D], F32)
        bhi = presb("bhi", [1, D], BF16)
        bhif = presb("bhif", [1, D], F32)
        blo = presb("blo", [1, D], BF16)
        ps_al = [pps(f"psal{i}", [1, 512]) for i in range(4)]
        ps_ab = pps("ps_ab", [128, 512])
        ps_f = [pps(f"psf{i}", [128, 512]) for i in range(3)]

        sgn_act, al_mm = {}, {}
        for i in range(NI):
            w = [] if i < 4 else [(s_act, sgn_act[i - 4])]
            p.em("sync", w, lambda e, i=i: e.dma_start(
                out=wf_f[i % 4][:], in_=wfT[i * 128:(i + 1) * 128, :]),
                inc=s_dx, k=16)
            ld = s_dx.n
            p.em("scalar", [(s_dx, ld)], lambda e, i=i: e.sign(
                out=sgn_bf[:, i, :], in_=wf_f[i % 4][:]))
            w = [] if i < 4 else [(s_pe, al_mm[i - 4])]
            p.em("scalar", w, lambda e, i=i: e.activation(
                out=absw[i % 4][:], in_=wf_f[i % 4][:], func=AF.Abs), inc=s_act)
            sgn_act[i] = s_act.n
            p.em("tensor", [(s_act, sgn_act[i])], lambda e, i=i: [
                e.matmul(ps_al[oc][:], ones_col[:],
                         absw[i % 4][:, oc * 512:(oc + 1) * 512],
                         start=(i == 0), stop=(i == NI - 1)) for oc in range(4)][-1],
                inc=s_pe)
            al_mm[i] = s_pe.n
        alpha_mm_done = s_pe.n
        p.em("sync", [], lambda e: e.dma_start(out=brow[:], in_=fbias),
             inc=s_dx, k=16)
        brow_ld = s_dx.n
        for oc in range(4):
            p.em("vector", [(s_pe, alpha_mm_done)], lambda e, oc=oc: e.tensor_scalar(
                out=arow[0:1, oc * 512:(oc + 1) * 512], in0=ps_al[oc][:],
                scalar1=1.0 / D, scalar2=None, op0=OP.mult),
                inc=s_dve if oc == 3 else None)
        p.em("vector", [(s_dx, brow_ld), (s_dve, s_dve.n)], lambda e: e.nop(),
             inc=s_dve)
        rows_done = s_dve.n
        for srow, dst in ((arow, alphab), (brow, biasb)):
            for q in (0, 32, 64, 96):
                p.em("vector", [(s_dve, rows_done)],
                     lambda e, srow=srow, dst=dst, q=q: e.tensor_copy(
                         out=dst[q:q + 1, :], in_=srow[:]),
                     inc=s_dve if q == 96 else None)
            p.em("vector", [(s_dve, s_dve.n)], lambda e, dst=dst:
                 e.stream_shuffle(out=dst[:], in_=dst[:], mask=[0] * 32),
                 inc=s_dve)
        p.em("vector", [(s_dve, s_dve.n)], lambda e: e.nop(), inc=s_act,
             k=1)
        bcast_done = s_act.n
        if "p6dbg" in taps:
            p.em("gpsimd", [(s_act, bcast_done)], lambda e: e.dma_start(
                out=dbg_alpha, in_=alphab[:]), inc=s_do, k=16)
            p.em("gpsimd", [], lambda e: e.dma_start(
                out=dbg_bias, in_=biasb[:]), inc=s_do, k=16)
            p.em("gpsimd", [], lambda e: e.dma_start(
                out=dbg_sgn, in_=sgn_bf[:]), inc=s_do, k=16)
        if stop == "p6pre":
            p.em("sync", [(s_do, s_do.n)], lambda e: None)
            p.run()
            pre_stack.close()
            ph.close()
            est.close()
            return nc
        pre_stack.close()

        x3t = [psb(f"x3t{i}", [128, NI, 128], BF16) for i in range(2)]
        x1t = [psb(f"x1t{i}", [128, D], F32) for i in range(2)]
        f_sb = [psb(f"fsb{i}", [128, D], F32) for i in range(2)]
        o_sb = [psb(f"osb{i}", [128, D], F32) for i in range(2)]
        dum6 = psb("dum6", [128, D], BF16)
        st_s1 = psb("p6s1", [128, 1], F32)
        st_s2 = psb("p6s2", [128, 1], F32)
        st24 = [psb(f"p6st{i}", [128, 24], F32) for i in range(2)]
        mv6 = [psb(f"p6mv{i}", [128, 2], F32) for i in range(2)]
        mrow = [psb(f"p6m{i}", [128, 1], F32) for i in range(2)]
        vrow = [psb(f"p6v{i}", [128, 1], F32) for i in range(2)]
        rrow = [psb(f"p6r{i}", [128, 1], F32) for i in range(2)]
        nmr = [psb(f"p6n{i}", [128, 1], F32) for i in range(2)]

        f_ev, f_st, add_ord, sdn, rdn, bnst_ord = {}, {}, {}, {}, {}, {}

        def ln4_b(tj):
            par = tj % 2
            p.em("vector", [(s_dve, bnst_ord[tj])], lambda e, par=par:
                 e.bn_aggr(out=mv6[par][:], in_=st24[par][:]), inc=s_dve)
            agg = s_dve.n
            p.em("gpsimd", [(s_dve, agg)], lambda e, par=par: e.tensor_scalar(
                out=vrow[par][:], in0=mv6[par][:, 1:2], scalar1=1.0,
                scalar2=EPS, op0=OP.mult, op1=OP.add), inc=s_pool)
            sdn[tj] = s_pool.n
            p.em("scalar", [(s_pool, sdn[tj])], lambda e, par=par: e.activation(
                out=vrow[par][:], in_=vrow[par][:], func=AF.Ln))
            p.em("scalar", [], lambda e, par=par: e.activation(
                out=rrow[par][:], in_=vrow[par][:], func=AF.Exp, scale=-0.5),
                inc=s_act)
            rdn[tj] = s_act.n

        def ln4_c(tj):
            par = tj % 2
            wex = [(s_act, rdn[tj])]
            if "p6dbg" in taps and tj == 0:
                p.em("gpsimd", [(s_dve, f_ev[("d", 0, 3)])], lambda e: e.dma_start(
                    out=dbg_f, in_=f_sb[0][:]), inc=s_do, k=16)
                wex = [(s_do, s_do.n)]
            p.em("vector", wex, lambda e, par=par: e.tensor_scalar(
                out=f_sb[par][:], in0=f_sb[par][:], scalar1=mv6[par][:, 0:1],
                scalar2=rrow[par][:], op0=OP.subtract, op1=OP.mult))
            w = [] if tj < 2 else [(s_do, f_st[tj - 2])]
            p.em("vector", w, lambda e, par=par: e.tensor_tensor(
                out=o_sb[par][:], in0=f_sb[par][:], in1=x1t[par][:], op=OP.add),
                inc=s_dve)
            odone = s_dve.n
            add_ord[tj] = odone
            p.em("gpsimd", [(s_dve, odone)], lambda e, tj=tj, par=par:
                 e.dma_start(out=out_own[tj * 128:(tj + 1) * 128, :],
                             in_=o_sb[par][:]), inc=s_do, k=16)
            f_st[tj] = s_do.n

        for tc in range(16):
            par = tc % 2
            w = [(s_act, bcast_done)] if tc < 2 else [(s_pe, f_ev[tc - 2])]
            p.em("sync", w, lambda e, tc=tc, par=par: e.dma_start_transpose(
                out=x3t[par][:], in_=xn3_scr[tc * 128:(tc + 1) * 128, :]),
                inc=s_dx, k=16)
            w = [] if tc < 2 else [(s_dve, add_ord[tc - 2])]
            p.em("sync", w, lambda e, tc=tc, par=par: e.dma_start(
                out=x1t[par][:], in_=x1_scr[tc * 128:(tc + 1) * 128, :]),
                inc=s_dx, k=16)
            xld = s_dx.n
            for oc in range(4):
                opar = (tc * 4 + oc) % 3
                w = [(s_dx, xld)]
                if tc == 0 and oc == 0:
                    w.append((s_act, bcast_done))
                if (tc * 4 + oc) >= 3:
                    pli = tc * 4 + oc - 3
                    w.append((s_dve, f_ev[("d", pli // 4, pli % 4)]))

                def fmms(e, tc=tc, par=par, opar=opar, oc=oc):
                    last = None
                    for i in range(NI):
                        last = e.matmul(ps_f[opar][:], x3t[par][:, i, :],
                                        sgn_bf[:, i, oc * 512:(oc + 1) * 512],
                                        start=(i == 0), stop=(i == NI - 1))
                    return last
                p.em("tensor", w, fmms, inc=s_pe)
                fm = s_pe.n
                if oc == 3:
                    f_ev[tc] = fm
                p.em("vector", [(s_pe, fm)], lambda e, par=par, opar=opar, oc=oc:
                     e.tensor_tensor(out=f_sb[par][:, oc * 512:(oc + 1) * 512],
                                     in0=ps_f[opar][:],
                                     in1=alphab[:, oc * 512:(oc + 1) * 512],
                                     op=OP.mult), inc=s_dve)
                f_ev[("d", tc, oc)] = s_dve.n
                p.em("gpsimd", [(s_dve, f_ev[("d", tc, oc)])],
                     lambda e, par=par, oc=oc: e.tensor_tensor(
                         out=f_sb[par][:, oc * 512:(oc + 1) * 512],
                         in0=f_sb[par][:, oc * 512:(oc + 1) * 512],
                         in1=biasb[:, oc * 512:(oc + 1) * 512], op=OP.add),
                     inc=s_pool)
                f_ev[("b", tc, oc)] = s_pool.n
                p.em("vector", [(s_pool, f_ev[("b", tc, oc)])],
                     lambda e, par=par, oc=oc: e.bn_stats(
                         out=st24[par][:, oc * 6:(oc + 1) * 6],
                         in_=f_sb[par][:, oc * 512:(oc + 1) * 512]),
                     inc=s_dve if oc == 3 else None)
                if oc == 3:
                    bnst_ord[tc] = s_dve.n
                if tc >= 1:
                    if oc == 0:
                        ln4_b(tc - 1)
                    elif oc == 1:
                        ln4_c(tc - 1)
        ln4_b(15)
        ln4_c(15)
        p.em("sync", [(s_do, s_do.n)], lambda e: None)
        p.run()
    est.close()
    return nc


_NC_CACHE = {}


def _get_nc():
    if "nc" not in _NC_CACHE:
        _NC_CACHE["nc"] = build_nc()
    return _NC_CACHE["nc"]


def _rope_tables():
    af = (1.0 / 1024.0) ** np.linspace(0.0, 1.0, 32, dtype=np.float32)
    t = np.arange(T, dtype=np.float32)
    theta = t[:, None] * af[None, :]
    cos = np.cos(theta).T.astype(np.float32)
    sin = np.sin(theta).T.astype(np.float32)
    C = np.ones((128, T), np.float32)
    S = np.zeros((128, T), np.float32)
    C[0:32] = cos
    C[64:96] = cos
    S[0:32] = sin
    S[64:96] = -sin
    return C, S


def kernel(L0_attn_x, L0_attn_Wq, L0_attn_Wk, L0_attn_Wv, L0_attn_Wo,
           L0_ffn_weight, L0_ffn_bias):
    x = np.ascontiguousarray(np.asarray(L0_attn_x, np.float32))
    wq = np.asarray(L0_attn_Wq, np.float32)
    wk = np.asarray(L0_attn_Wk, np.float32)
    wv = np.asarray(L0_attn_Wv, np.float32)
    wo = np.asarray(L0_attn_Wo, np.float32)
    wf = np.asarray(L0_ffn_weight, np.float32)
    fb = np.asarray(L0_ffn_bias, np.float32).reshape(1, D)

    wqT = np.ascontiguousarray(wq.T)
    wkT = np.ascontiguousarray(wk.T)
    wvT = np.ascontiguousarray(wv.T)
    woT = np.ascontiguousarray(wo.T)
    wfT = np.ascontiguousarray(wf.T)

    C, S = _rope_tables()
    diag = np.triu(np.ones((128, 128), np.float32)).astype(ml_dtypes.bfloat16)
    ones_m = np.ones((128, 128), np.float32).astype(ml_dtypes.bfloat16)
    zeros_m = np.zeros((128, 128), np.float32).astype(ml_dtypes.bfloat16)
    ident = np.eye(128, dtype=np.float32).astype(ml_dtypes.bfloat16)

    nc = _get_nc()
    in_maps = []
    for core in range(8):
        b, hh = core // 2, core % 2
        xt = x[b].reshape(32, 128, D)
        own_tiles = list(range(hh, 32, 2))
        x_own_c = np.ascontiguousarray(xt[own_tiles].reshape(T // 2, D))
        own_cols = np.concatenate(
            [np.arange(t * 128, (t + 1) * 128) for t in own_tiles])
        in_maps.append({
            "x_full": x[b],
            "x_own": x_own_c,
            "wqT": wqT, "wkT": wkT, "wvT": wvT, "woT": woT, "wfT": wfT,
            "fbias": fb,
            "ropeC": C, "ropeS": S,
            "ropeCq": np.ascontiguousarray(C[:, own_cols]),
            "ropeSq": np.ascontiguousarray(S[:, own_cols]),
            "maskP_in": diag if hh == 0 else ones_m,
            "maskQ_in": zeros_m if hh == 0 else diag,
            "ident_in": ident,
        })
    res = run_bass_kernel_spmd(nc, in_maps, core_ids=list(range(8)))
    out = np.empty((4, T, D), np.float32)
    for core in range(8):
        b, hh = core // 2, core % 2
        shard = np.asarray(res.results[core]["out_own"]).reshape(16, 128, D)
        out[b].reshape(32, 128, D)[list(range(hh, 32, 2))] = shard
    return out


# version 20 (final)
# speedup vs baseline: 1.1589x; 1.1589x over previous
import contextlib
import math
import numpy as np
import ml_dtypes
import concourse.bass as bass
import concourse.mybir as mybir
from concourse.bass_utils import run_bass_kernel_spmd

F32 = mybir.dt.float32
BF16 = mybir.dt.bfloat16
AF = mybir.ActivationFunctionType
OP = mybir.AluOpType
AX = mybir.AxisListType.X

T = 4096
D = 2048
H = 16
DH = 128
NI = 16
SCALE = 1.0 / math.sqrt(128.0)
EPS = 1e-5
THIRD = 1.0 / 3.0

ENGINES = ("sync", "scalar", "vector", "tensor", "gpsimd")
DBG_SKIP = set()
DBG_LN_TILE = [0]


class Sem:
    def __init__(self, nc, name, stack):
        self.handle = stack.enter_context(nc.semaphore(name))
        self.n = 0


class Phase:

    def __init__(self, nc, name):
        self.nc = nc
        self.name = name
        self.ops = {e: [] for e in ENGINES}

    def em(self, e, waits, fn, inc=None, k=1):
        if inc is not None:
            inc.n += k
        self.ops[e].append((tuple(waits), fn, inc, k))

    def run(self):
        def body(eng, lst):
            for waits, fn, inc, k in lst:
                for (s, v) in waits:
                    eng.wait_ge(s.handle, v)
                inst = fn(eng)
                if inc is not None:
                    inst.then_inc(inc.handle, k)

        with self.nc.Block(self.name) as blk:
            for e in ENGINES:
                if self.ops[e]:
                    getattr(blk, e)(lambda eng, lst=self.ops[e]: body(eng, lst))


def build_nc(stop=None, taps=()):
    lndbg_state = []
    nc = bass.Bass("TRN2", target_bir_lowering=False)

    x_full = nc.dram_tensor("x_full", [T, D], F32, kind="ExternalInput").ap()
    x_own = nc.dram_tensor("x_own", [T // 2, D], F32, kind="ExternalInput").ap()
    wqT = nc.dram_tensor("wqT", [H, 128, NI, 128], F32, kind="ExternalInput").ap()
    wkT = nc.dram_tensor("wkT", [H, 128, NI, 128], F32, kind="ExternalInput").ap()
    wvT = nc.dram_tensor("wvT", [D, D], F32, kind="ExternalInput").ap()
    woT = nc.dram_tensor("woT", [D, D], F32, kind="ExternalInput").ap()
    wfT = nc.dram_tensor("wfT", [D, D], F32, kind="ExternalInput").ap()
    fbias = nc.dram_tensor("fbias", [1, D], F32, kind="ExternalInput").ap()
    ropeC = nc.dram_tensor("ropeC", [128, T], F32, kind="ExternalInput").ap()
    ropeS = nc.dram_tensor("ropeS", [128, T], F32, kind="ExternalInput").ap()
    ropeCq = nc.dram_tensor("ropeCq", [128, T // 2], F32, kind="ExternalInput").ap()
    ropeSq = nc.dram_tensor("ropeSq", [128, T // 2], F32, kind="ExternalInput").ap()
    maskP_in = nc.dram_tensor("maskP_in", [128, 128], BF16, kind="ExternalInput").ap()
    maskQ_in = nc.dram_tensor("maskQ_in", [128, 128], BF16, kind="ExternalInput").ap()
    ident_in = nc.dram_tensor("ident_in", [128, 128], BF16, kind="ExternalInput").ap()

    out_own = nc.dram_tensor("out_own", [T // 2, D], F32, kind="ExternalOutput").ap()
    if "lndbg" in taps:
        dbg_ln = nc.dram_tensor("dbg_ln", [128, 8], F32, kind="ExternalOutput").ap()
    if "xn1dbg" in taps:
        dbg_xn1 = nc.dram_tensor("dbg_xn1", [128, NI, T // 2], BF16, kind="ExternalOutput").ap()
    if "p6dbg" in taps:
        dbg_alpha = nc.dram_tensor("dbg_alpha", [128, D], F32, kind="ExternalOutput").ap()
        dbg_bias = nc.dram_tensor("dbg_bias", [128, D], F32, kind="ExternalOutput").ap()
        dbg_sgn = nc.dram_tensor("dbg_sgn", [128, NI, D], BF16, kind="ExternalOutput").ap()
        dbg_f = nc.dram_tensor("dbg_f", [128, D], F32, kind="ExternalOutput").ap()

    def scr_kind(n):
        return "ExternalOutput" if n in taps else "Internal"
    k_scr = nc.dram_tensor("k_scr", [H, DH, T], BF16, kind=scr_kind("k_scr")).ap()
    q_scr = nc.dram_tensor("q_scr", [H, DH, T // 2], BF16, kind=scr_kind("q_scr")).ap()
    v_scr = nc.dram_tensor("v_scr", [T, D], BF16, kind=scr_kind("v_scr")).ap()
    x1_scr = nc.dram_tensor("x1_scr", [T // 2, D], F32, kind=scr_kind("x1_scr")).ap()
    xn3_scr = nc.dram_tensor("xn3_scr", [T // 2, D], BF16, kind=scr_kind("xn3_scr")).ap()

    est = contextlib.ExitStack()

    s_dx = Sem(nc, "s_dx", est)
    s_dc = Sem(nc, "s_dc", est)
    s_dh = Sem(nc, "s_dh", est)
    s_dk = Sem(nc, "s_dk", est)
    s_do = Sem(nc, "s_do", est)
    s_dt = Sem(nc, "s_dt", est)
    s_pe = Sem(nc, "s_pe", est)
    s_act = Sem(nc, "s_act", est)
    s_dve = Sem(nc, "s_dve", est)
    s_pool = Sem(nc, "s_pool", est)
    ALL_SEMS = (s_dx, s_dc, s_dh, s_dk, s_do, s_dt, s_pe, s_act, s_dve, s_pool)

    def barrier(tag):
        p = Phase(nc, "bar_" + tag)
        snap = [(s, s.n) for s in ALL_SEMS if s.n > 0]
        for e in ENGINES:
            p.em(e, snap, lambda eng: None)
        p.run()

    def sb(name, shape, dtype):
        return est.enter_context(nc.sbuf_tensor(name, shape, dtype))

    def _fin():
        pf = Phase(nc, "fin")
        snap = [(s, s.n) for s in ALL_SEMS if s.n > 0]
        pf.em("gpsimd", snap, lambda e: e.dma_start(
            out=out_own[0:128, :128], in_=maskP_sb[:]), inc=s_do, k=16)
        pf.em("sync", [(s_do, s_do.n)], lambda e: None)
        pf.run()
        return nc

    maskP_sb = sb("maskP_sb", [128, 128], BF16)
    maskQ_sb = sb("maskQ_sb", [128, 128], BF16)
    ident_sb = sb("ident_sb", [128, 128], BF16)
    ones_col = sb("ones_col", [128, 1], BF16)
    ones_row = sb("ones_row", [1, 128], BF16)
    mk2 = sb("mk2", [1, H], F32)
    mq2 = sb("mq2", [1, H], F32)
    mhat_sb = sb("mhat_sb", [128, H], F32)
    lndbg_sb = sb("lndbg_sb", [128, 8], F32) if "lndbg" in taps else None

    p = Phase(nc, "ph0")
    for dst, src in ((maskP_sb, maskP_in), (maskQ_sb, maskQ_in),
                     (ident_sb, ident_in)):
        p.em("sync", [], lambda e, d=dst, s=src: e.dma_start(out=d[:], in_=s),
             inc=s_dc, k=16)
    for tile, val in ((ones_col, 1.0), (ones_row, 1.0), (mk2, 0.0), (mq2, 0.0)):
        p.em("gpsimd", [], lambda e, t=tile, v=val: e.memset(t[:], v), inc=s_pool)
    p.run()

    prod_stack = contextlib.ExitStack()

    def prod_sb(name, shape, dtype):
        return prod_stack.enter_context(nc.sbuf_tensor(name, shape, dtype))

    xn1T = prod_sb("xn1T", [128, NI, T // 2], BF16)
    xn1To = prod_sb("xn1To", [128, NI, T // 4], BF16)
    consts_ready = s_dc.n

    def ln1_and_v(hf, dgpair, wv_bufs):
        tbase = hf * 16
        first = dgpair[0] == 0
        p = Phase(nc, f"p1_{hf}_{dgpair[0]}")
        with contextlib.ExitStack() as ph:
            sfx = f"_{hf}_{dgpair[0]}"

            def psb(name, shape, dtype):
                return ph.enter_context(nc.sbuf_tensor(name + sfx, shape, dtype))

            def pps(name, shape, dtype=F32):
                return ph.enter_context(nc.psum_tensor(name + sfx, shape, dtype))

            wv_f = [psb(f"p1wf{i}", [128, 2, 512], F32) for i in range(2)]
            wv_bf = wv_bufs[0 if first else 1]
            vst = [[psb(f"p1vs{i}{dg}", [128, 512], BF16) for dg in range(2)]
                   for i in range(2)]
            ps_v = [[pps(f"p1pv{i}{dg}", [128, 512], F32) for dg in range(2)]
                    for i in range(2)]
            if first:
                xs = [psb(f"p1x{i}", [128, D], F32) for i in range(2)]
                st24 = [psb(f"p1st{i}", [128, 24], F32) for i in range(2)]
                mv6 = [psb(f"p1mv{i}", [128, 2], F32) for i in range(2)]
                vrow = [psb(f"p1v{i}", [128, 1], F32) for i in range(2)]
                rrow = [psb(f"p1r{i}", [128, 1], F32) for i in range(2)]
                xn1 = [psb(f"p1xn{i}", [128, D], BF16) for i in range(2)]

            wv_cast = {}

            def emit_wv(pc):
                pl = pc % 16
                dgi, q8 = pl // 8, pl % 8
                dg = dgpair[dgi] + (2 if pc >= 16 else 0)
                dst = wv_bufs[1 if pc >= 16 else 0]
                w = [(s_dc, consts_ready)] if pc == 0 else []
                if pc >= 2:
                    w.append((s_act, wv_cast[pc - 2]))
                p.em("sync", w, lambda e, dg=dg, q8=q8, pc=pc: e.dma_start(
                    out=wv_f[pc % 2][:],
                    in_=wvT.rearrange("(a p) o -> p a o", p=128)[
                        :, q8 * 2:(q8 + 1) * 2, dg * 512:(dg + 1) * 512]),
                    inc=s_dx, k=16)
                ld = s_dx.n
                p.em("scalar", [(s_dx, ld)], lambda e, dgi=dgi, q8=q8, pc=pc,
                     dst=dst: e.copy(out=dst[:].rearrange(
                         "p (a b) o -> p a b o", a=8)[
                             :, q8, :, dgi * 512:(dgi + 1) * 512],
                         in_=wv_f[pc % 2][:]), inc=s_act)
                wv_cast[pc] = s_act.n

            ntiles = 24 if first else 16
            ord_apply, ord_trd, ord_v, ord_evv, ord_vst = {}, {}, {}, {}, {}

            def emit_vmms(st):
                w = [(s_act, wv_cast[15]) if first else wv_ready]
                if first:
                    w.append((s_dt, ord_trd[st]))
                if st >= 2:
                    w.append((s_dve, ord_evv[st - 2]))

                def vmms(e, st=st):
                    last = None
                    for dgi in range(2):
                        for i in range(NI):
                            last = e.matmul(
                                ps_v[st % 2][dgi][:],
                                xn1T[:, i, st * 128:(st + 1) * 128],
                                wv_bf[:, i, dgi * 512:(dgi + 1) * 512],
                                start=(i == 0), stop=(i == NI - 1))
                    return last
                p.em("tensor", w, vmms, inc=s_pe)
                ord_v[st] = s_pe.n
                w = [(s_pe, ord_v[st])]
                if st >= 2:
                    w.append((s_dk, ord_vst[st - 2]))
                p.em("vector", w, lambda e, st=st: e.tensor_copy(
                    out=vst[st % 2][0][:], in_=ps_v[st % 2][0][:]))
                p.em("vector", [], lambda e, st=st: e.tensor_copy(
                    out=vst[st % 2][1][:], in_=ps_v[st % 2][1][:]), inc=s_dve)
                ord_evv[st] = s_dve.n
                ts = tbase + st
                for dgi in range(2):
                    dg = dgpair[dgi]
                    p.em("gpsimd",
                         [(s_dve, ord_evv[st])] if dgi == 0 else [],
                         lambda e, ts=ts, st=st, dgi=dgi, dg=dg: e.dma_start(
                             out=v_scr[ts * 128:(ts + 1) * 128,
                                       dg * 512:(dg + 1) * 512],
                             in_=vst[st % 2][dgi][:]), inc=s_dk, k=16)
                ord_vst[st] = s_dk.n

            wv_ready = (s_act, wv_bufs[2]) if not first else None
            for st in range(ntiles):
                own = st >= 16
                par = st % 2
                ts = tbase + st
                if first:
                    if st < 16:
                        emit_wv(st)
                    else:
                        emit_wv(16 + (st - 16) * 2)
                        emit_wv(17 + (st - 16) * 2)
                if first:
                    if own:
                        oj = hf * 8 + (st - 16)
                        src_ap = x_own[oj * 128:(oj + 1) * 128, :]
                    else:
                        src_ap = x_full[ts * 128:(ts + 1) * 128, :]
                    w = [] if st < 2 else [(s_dve, ord_apply[st - 2])]
                    p.em("gpsimd", w, lambda e, src_ap=src_ap, par=par: e.dma_start(
                        out=xs[par][:], in_=src_ap), inc=s_dk, k=16)
                    xld = s_dk.n
                    for c4 in range(4):
                        p.em("vector", [(s_dk, xld)] if c4 == 0 else [],
                             lambda e, par=par, c4=c4: e.bn_stats(
                                 out=st24[par][:, c4 * 6:(c4 + 1) * 6],
                                 in_=xs[par][:, c4 * 512:(c4 + 1) * 512]),
                             inc=s_dve if c4 == 3 else None)
                    bnd = s_dve.n
                    p.em("vector", [(s_dve, bnd)], lambda e, par=par:
                         e.bn_aggr(out=mv6[par][:], in_=st24[par][:]),
                         inc=s_dve)
                    agg = s_dve.n
                    p.em("gpsimd", [(s_dve, agg)], lambda e, par=par:
                         e.tensor_scalar(
                             out=vrow[par][:], in0=mv6[par][:, 1:2],
                             scalar1=1.0, scalar2=EPS, op0=OP.mult,
                             op1=OP.add), inc=s_pool)
                    stats_done = s_pool.n
                    p.em("scalar", [(s_pool, stats_done)], lambda e, par=par:
                         e.activation(out=vrow[par][:], in_=vrow[par][:],
                                      func=AF.Ln))
                    p.em("scalar", [], lambda e, par=par: e.activation(
                        out=rrow[par][:], in_=vrow[par][:], func=AF.Exp,
                        scale=-0.5), inc=s_act)
                    rstd_done = s_act.n
                    w = [(s_act, rstd_done)]
                    if st >= 2:
                        w.append((s_dt, ord_trd[st - 2]))
                    p.em("vector", w, lambda e, par=par: e.tensor_scalar(
                        out=xn1[par][:], in0=xs[par][:],
                        scalar1=mv6[par][:, 0:1],
                        scalar2=rrow[par][:], op0=OP.subtract, op1=OP.mult),
                        inc=s_dve)
                    ord_apply[st] = s_dve.n
                    dstT = xn1To if own else xn1T
                    dcol = (st - 16) * 128 if own else st * 128
                    p.em("scalar", [(s_dve, ord_apply[st])], lambda e, par=par,
                         dstT=dstT, dcol=dcol: e.dma_start_transpose(
                             out=dstT[:, 0:16, dcol:dcol + 128],
                             in_=xn1[par][:]), inc=s_dt, k=16)
                    ord_trd[st] = s_dt.n
                if not first:
                    emit_vmms(st)
                elif st >= 16:
                    emit_vmms(st - 16)
            if first:
                for j in range(8, 16):
                    emit_vmms(j)
            p.run()
            return wv_cast.get(31)

    def kq_phase(hf, which):
        is_k = which == "k"
        wsrc = wkT if is_k else wqT
        dscr = k_scr if is_k else q_scr
        mdst = mk2 if is_k else mq2
        srcT = xn1T if is_k else xn1To
        ngrp = 4 if is_k else 2
        p = Phase(nc, f"p{which}_{hf}")
        with contextlib.ExitStack() as ph:
            def psb(name, shape, dtype):
                return ph.enter_context(
                    nc.sbuf_tensor(name + which + str(hf), shape, dtype))

            def pps(name, shape, dtype=F32):
                return ph.enter_context(
                    nc.psum_tensor(name + which + str(hf), shape, dtype))

            if is_k:
                cs_tab = (psb("ropeCk_sb", [128, T], F32),
                          psb("ropeSk_sb", [128, T], F32))
                rope_srcs = ((cs_tab[0], ropeC), (cs_tab[1], ropeS))
            else:
                cs_tab = (psb("ropeCq_sb", [128, T // 2], F32),
                          psb("ropeSq_sb", [128, T // 2], F32))
                rope_srcs = ((cs_tab[0], ropeCq), (cs_tab[1], ropeSq))
            for dst, src2 in rope_srcs:
                p.em("sync", [], lambda e, d=dst, s=src2: e.dma_start(
                    out=d[:], in_=s), inc=s_dx, k=16)
            cs_ld = s_dx.n

            wf = [psb(f"wf{i}", [128, NI, 128], F32) for i in range(3)]
            wbf = [psb(f"wbf{i}", [128, NI, 128], BF16) for i in range(3)]
            kswap = [psb(f"ksw{i}", [128, 512], F32) for i in range(2)]
            t1 = psb("t1", [128, 512], F32)
            t2 = psb("t2", [128, 512], F32)
            kbf = [psb(f"kbf{i}", [128, 512], BF16) for i in range(4)]
            sq = [psb(f"sq{i}", [128, 512], BF16) for i in range(4)]
            r1 = psb("r1", [1, 1], F32)
            ps_k = [pps(f"psk{i}", [128, 512]) for i in range(4)]
            ps_k2 = [pps(f"psk2{i}", [1, 512]) for i in range(2)]

            for i in range(2):
                p.em("gpsimd", [], lambda e, i=i: e.memset(kswap[i][32:64, :], 0.0))
                p.em("gpsimd", [], lambda e, i=i: e.memset(kswap[i][64:96, :], 0.0),
                     inc=s_pool)
            swz = s_pool.n

            nli = H * ngrp
            wcast, mm_last = {}, {}
            o_dve, o_swp, o_str, o_sq, o_red, k2_ord = {}, {}, {}, {}, {}, {}

            def emit_k2(lj):
                par = lj % 4
                kpar = lj % 2
                hj = lj // ngrp
                w = [(s_pool, o_sq[lj])]
                if lj >= 2:
                    w.append((s_dve, o_red[lj - 2]))
                p.em("tensor", w, lambda e, par=par, kpar=kpar: e.matmul(
                    ps_k2[kpar][:], ones_col[:], sq[par][:],
                    start=True, stop=True), inc=s_pe)
                k2_ord[lj] = s_pe.n
                p.em("vector", [(s_pe, k2_ord[lj])], lambda e, kpar=kpar:
                     e.reduce_max(out=r1[:], in_=ps_k2[kpar][:], axis=AX),
                     inc=s_dve)
                p.em("vector", [(s_dve, s_dve.n)], lambda e, hj=hj, mdst=mdst:
                     e.tensor_tensor(out=mdst[0:1, hj:hj + 1],
                                     in0=mdst[0:1, hj:hj + 1],
                                     in1=r1[:], op=OP.max), inc=s_dve)
                o_red[lj] = s_dve.n

            for h in range(H):
                wpar = h % 3
                for piece in range(2):
                    w = []
                    if piece == 0 and h >= 3:
                        w.append((s_act, wcast[h - 3]))
                    p.em("sync", w, lambda e, h=h, wpar=wpar, piece=piece:
                         e.dma_start(
                             out=wf[wpar][:, piece * 8:(piece + 1) * 8, :],
                             in_=wsrc[h, :, piece * 8:(piece + 1) * 8, :]),
                         inc=s_dx, k=16)
                wld = s_dx.n
                w = [(s_dx, wld)]
                if h >= 3:
                    w.append((s_pe, mm_last[h - 3]))
                p.em("scalar", w, lambda e, wpar=wpar: e.copy(
                    out=wbf[wpar][:], in_=wf[wpar][:]), inc=s_act)
                wcast[h] = s_act.n

                for g in range(ngrp):
                    gg = hf * ngrp + g
                    li = h * ngrp + g
                    par = li % 4
                    spar = li % 2
                    w = [(s_act, wcast[h])]
                    if li >= 4:
                        w.append((s_dve, o_dve[li - 4]))
                        w.append((s_act, o_swp[li - 4]))

                    def mms(e, wpar=wpar, par=par, g=g, srcT=srcT):
                        last = None
                        for i in range(NI):
                            last = e.matmul(
                                ps_k[par][:], wbf[wpar][:, i, :],
                                srcT[:, i, g * 512:(g + 1) * 512],
                                start=(i == 0), stop=(i == NI - 1))
                        return last
                    p.em("tensor", w, mms, inc=s_pe)
                    mmord = s_pe.n
                    if g == ngrp - 1:
                        mm_last[h] = mmord
                    if li >= 2:
                        emit_k2(li - 2)
                    w = [(s_pe, mmord)]
                    if li >= 2:
                        w.append((s_dve, o_dve[li - 2]))
                    p.em("scalar", w, lambda e, par=par, spar=spar: e.copy(
                        out=kswap[spar][0:32, :], in_=ps_k[par][64:96, :]))
                    p.em("scalar", [], lambda e, par=par, spar=spar: e.copy(
                        out=kswap[spar][64:96, :], in_=ps_k[par][0:32, :]),
                        inc=s_act)
                    o_swp[li] = s_act.n
                    cs = slice(gg * 512, (gg + 1) * 512)
                    w = [(s_pe, mmord), (s_pool, swz), (s_dx, cs_ld),
                         (s_act, o_swp[li])]
                    if li >= 4:
                        w.append((s_dk, o_str[li - 4]))
                        w.append((s_pool, o_sq[li - 4]))
                    p.em("vector", w, lambda e, par=par, cs=cs, cs_tab=cs_tab:
                         e.tensor_tensor(out=t1[:], in0=ps_k[par][:],
                                         in1=cs_tab[0][:, cs], op=OP.mult))
                    p.em("vector", [], lambda e, spar=spar, cs=cs, cs_tab=cs_tab:
                         e.tensor_tensor(out=t2[:], in0=kswap[spar][:],
                                         in1=cs_tab[1][:, cs], op=OP.mult))
                    p.em("vector", [], lambda e, par=par: e.tensor_tensor(
                        out=kbf[par][:], in0=t1[:], in1=t2[:], op=OP.add),
                        inc=s_dve)
                    o_dve[li] = s_dve.n
                    p.em("gpsimd", [(s_dve, o_dve[li])], lambda e, par=par:
                         e.tensor_tensor(out=sq[par][:], in0=kbf[par][:],
                                         in1=kbf[par][:], op=OP.mult), inc=s_pool)
                    o_sq[li] = s_pool.n
                    p.em("gpsimd", [], lambda e, h=h, gg=gg, par=par, dscr=dscr:
                         e.dma_start(out=dscr[h, :, gg * 512:(gg + 1) * 512],
                                     in_=kbf[par][:]), inc=s_dk, k=16)
                    o_str[li] = s_dk.n
            emit_k2(nli - 2)
            emit_k2(nli - 1)
            p.run()

    for hf in range(2):
        p1s = contextlib.ExitStack()
        wv_bufs = [p1s.enter_context(
            nc.sbuf_tensor(f"p1wbf{hf}{i}", [128, NI, 1024], BF16))
            for i in range(2)]
        wv_bufs.append(None)
        wv_bufs[2] = ln1_and_v(hf, (0, 1), wv_bufs)
        barrier(f"va{hf}")
        if hf == 0 and "xn1dbg" in taps:
            pdb = Phase(nc, "xn1dbg")
            pdb.em("gpsimd", [], lambda e: e.dma_start(
                out=dbg_xn1, in_=xn1T[:]), inc=s_do, k=16)
            pdb.run()
            barrier("xn1dbg")
        if stop == f"va{hf}":
            return _fin()
        ln1_and_v(hf, (2, 3), wv_bufs)
        p1s.close()
        barrier(f"vb{hf}")
        if stop == f"vb{hf}":
            return _fin()
        kq_phase(hf, "k")
        barrier(f"k{hf}")
        if stop == f"k{hf}":
            return _fin()
        kq_phase(hf, "q")
        barrier(f"q{hf}")
        if stop == f"q{hf}":
            return _fin()

    prod_stack.close()

    p = Phase(nc, "mhat")
    with contextlib.ExitStack() as ph:
        prodr = ph.enter_context(nc.sbuf_tensor("prodr", [1, H], F32))
        mneg_bf = ph.enter_context(nc.sbuf_tensor("mneg_bf", [1, H], BF16))
        ps_mh = ph.enter_context(nc.psum_tensor("ps_mh", [128, H], F32))
        p.em("vector", [], lambda e: e.tensor_tensor(
            out=prodr[:], in0=mq2[:], in1=mk2[:], op=OP.mult), inc=s_dve)
        v0 = s_dve.n
        p.em("scalar", [(s_dve, v0)], lambda e: e.activation(
            out=prodr[:], in_=prodr[:], func=AF.Ln))
        p.em("scalar", [], lambda e: e.activation(
            out=prodr[:], in_=prodr[:], func=AF.Exp, scale=0.5), inc=s_act)
        a0 = s_act.n
        p.em("vector", [(s_act, a0)], lambda e: e.tensor_scalar(
            out=mneg_bf[:], in0=prodr[:], scalar1=-SCALE, scalar2=None,
            op0=OP.mult), inc=s_dve)
        v1 = s_dve.n
        p.em("tensor", [(s_dve, v1)], lambda e: e.matmul(
            ps_mh[:], ones_row[:], mneg_bf[:], start=True, stop=True), inc=s_pe)
        t0 = s_pe.n
        p.em("scalar", [(s_pe, t0)], lambda e: e.copy(
            out=mhat_sb[:], in_=ps_mh[:]), inc=s_act)
        p.run()
    barrier("mhat")
    if stop == "mhat":
        return _fin()

    attn_stack = contextlib.ExitStack()
    y_sb = attn_stack.enter_context(nc.sbuf_tensor("y_sb", [128, H, T // 2], BF16))
    wo_bf = attn_stack.enter_context(nc.sbuf_tensor("wobf", [128, NI, D], BF16))

    p = Phase(nc, "attn")
    with contextlib.ExitStack() as ph:
        def psb(name, shape, dtype):
            return ph.enter_context(nc.sbuf_tensor(name, shape, dtype))

        def pps(name, shape, dtype=F32):
            return ph.enter_context(nc.psum_tensor(name, shape, dtype))

        ksb = [psb(f"ksb{i}", [128, T], BF16) for i in range(2)]
        qsb = [psb(f"qsb{i}", [128, T // 2], BF16) for i in range(2)]
        vsb = [psb(f"vsb{i}", [128, 32, 128], BF16) for i in range(2)]
        pT = [psb(f"pT{i}", [128, 512], BF16) for i in range(8)]
        rlf = [psb(f"rlf{i}", [1, 512], F32) for i in range(2)]
        rlb_sb = [psb(f"rlb{i}", [128, 512], F32) for i in range(2)]
        ps_s = [pps(f"pss{i}", [128, 512]) for i in range(4)]
        ps_l = [pps(f"psl{i}", [1, 512]) for i in range(2)]
        ps_y = [pps(f"psy{i}", [128, 512]) for i in range(2)]

        wo_f = [psb(f"wof{i}", [128, 1, D], F32) for i in range(2)]
        hload, head_free, wo_cast = {}, {}, {}

        def emit_wo(pc):
            w = [] if pc < 2 else [(s_act, wo_cast[pc - 2])]
            p.em("gpsimd", w, lambda e, pc=pc: e.dma_start(
                out=wo_f[pc % 2][:],
                in_=woT.rearrange("(hc p) o -> p hc o", p=128)[
                    :, pc:pc + 1, :]), inc=s_do, k=16)
            ld = s_do.n
            p.em("scalar", [(s_do, ld)], lambda e, pc=pc: e.copy(
                out=wo_bf[:, pc:pc + 1, :], in_=wo_f[pc % 2][:]),
                inc=s_act)
            wo_cast[pc] = s_act.n

        exp_ord, pv_ord, msk_ord = {}, {}, {}
        yev_ord, recip_ord, rlbev_ord = {}, {}, {}

        def head_loads(h):
            par = h % 2
            w = [(s_pe, head_free[h - 2])] if h >= 2 else []
            p.em("sync", w, lambda e, h=h, par=par: e.dma_start(
                out=ksb[par][:], in_=k_scr[h]), inc=s_dh, k=16)
            p.em("sync", [], lambda e, h=h, par=par: e.dma_start(
                out=qsb[par][:], in_=q_scr[h]), inc=s_dh, k=16)
            for piece in range(4):
                p.em("sync", [], lambda e, h=h, par=par, piece=piece: e.dma_start(
                    out=vsb[par][:, piece * 8:(piece + 1) * 8, :],
                    in_=v_scr.rearrange("(sc p) d -> p sc d", p=128)[
                        :, piece * 8:(piece + 1) * 8, h * 128:(h + 1) * 128]),
                    inc=s_dh, k=16)
            hload[h] = s_dh.n
            if 2 <= h <= 9:
                emit_wo(2 * (h - 2))
                emit_wo(2 * (h - 2) + 1)

        head_loads(0)
        head_loads(1)

        blocks = []
        gb = 0
        for h in range(H):
            for j0 in (0, 4, 8, 12):
                grp = h * 4 + j0 // 4
                B = 2 * j0 + 8
                for bb in range(B):
                    blocks.append((h, j0, grp, bb, B, gb))
                    gb += 1
        NB = len(blocks)
        LAG = 3

        def emit_S(i):
            h, j0, grp, bb, B, gb = blocks[i]
            par = h % 2
            js = max(0, bb // 2 - j0)
            sslice = slice(js * 128, 512)
            qlo = j0 * 128 + js * 128
            qhi = (j0 + 4) * 128
            spar = gb % 4
            w = [(s_dh, hload[h])]
            if gb >= 4:
                w.append((s_act, exp_ord[gb - 4]))
            p.em("tensor", w, lambda e, par=par, spar=spar, bb=bb,
                 sslice=sslice, qlo=qlo, qhi=qhi: e.matmul(
                     ps_s[spar][:, sslice],
                     ksb[par][:, bb * 128:(bb + 1) * 128],
                     qsb[par][:, qlo:qhi], start=True, stop=True), inc=s_pe)
            smm = s_pe.n
            w = [(s_pe, smm)]
            if gb >= 8:
                w.append((s_pe, pv_ord[gb - 8]))
            p.em("scalar", w, lambda e, h=h, spar=spar, gb=gb,
                 sslice=sslice: e.activation(
                     out=pT[gb % 8][:, sslice], in_=ps_s[spar][:, sslice],
                     func=AF.Exp, bias=mhat_sb[:, h:h + 1], scale=SCALE),
                 inc=s_act)
            exp_ord[gb] = s_act.n
            diag = bb >= 2 * j0
            if diag:
                mtile = maskP_sb if bb % 2 == 0 else maskQ_sb
                p.em("gpsimd", [(s_act, exp_ord[gb])],
                     lambda e, gb=gb, js=js, mtile=mtile: e.tensor_tensor(
                         out=pT[gb % 8][:, js * 128:(js + 1) * 128],
                         in0=pT[gb % 8][:, js * 128:(js + 1) * 128],
                         in1=mtile[:], op=OP.mult), inc=s_pool)
                msk_ord[gb] = s_pool.n

        def emit_lpv(i):
            h, j0, grp, bb, B, gb = blocks[i]
            par = h % 2
            gpar = grp % 2
            js = max(0, bb // 2 - j0)
            sslice = slice(js * 128, 512)
            stop = (bb == B - 1)
            start = (bb == 0)
            diag = bb >= 2 * j0
            w = [(s_act, exp_ord[gb])]
            if diag:
                w.append((s_pool, msk_ord[gb]))
            if start and grp >= 2:
                w.append((s_dve, recip_ord[grp - 2]))
            p.em("tensor", w, lambda e, gpar=gpar, gb=gb, start=start,
                 stop=stop, sslice=sslice: e.matmul(
                     ps_l[gpar][:, sslice], ones_col[:],
                     pT[gb % 8][:, sslice], start=start, stop=stop))
            w = []
            if start and grp >= 2:
                w.append((s_dve, yev_ord[grp - 2]))
            p.em("tensor", w, lambda e, par=par, gpar=gpar, gb=gb, bb=bb,
                 start=start, stop=stop, sslice=sslice: e.matmul(
                     ps_y[gpar][:, sslice], vsb[par][:, bb, :],
                     pT[gb % 8][:, sslice], start=start, stop=stop), inc=s_pe)
            pv_ord[gb] = s_pe.n
            if stop:
                epi_a(h, j0, grp, gb)

        pend = {}

        def epi_a(h, j0, grp, gb):
            gpar = grp % 2
            p.em("vector", [(s_pe, pv_ord[gb])], lambda e, gpar=gpar:
                 e.reciprocal(out=rlf[gpar][:], in_=ps_l[gpar][:]), inc=s_dve)
            recip_ord[grp] = s_dve.n
            for q in (0, 32, 64, 96):
                p.em("vector", [(s_dve, recip_ord[grp])],
                     lambda e, gpar=gpar, q=q: e.tensor_copy(
                         out=rlb_sb[gpar][q:q + 1, :], in_=rlf[gpar][:]),
                     inc=s_dve if q == 96 else None)
            seed_ord = s_dve.n
            p.em("vector", [(s_dve, seed_ord)], lambda e, gpar=gpar:
                 e.stream_shuffle(out=rlb_sb[gpar][:], in_=rlb_sb[gpar][:],
                                  mask=[0] * 32), inc=s_dve)
            rlbev_ord[grp] = s_dve.n
            p.em("vector", [(s_dve, rlbev_ord[grp])], lambda e, h=h, j0=j0,
                 gpar=gpar: e.scalar_tensor_tensor(
                     out=y_sb[:, h, j0 * 128:(j0 + 4) * 128], in0=ps_y[gpar][:],
                     scalar=THIRD, in1=rlb_sb[gpar][:],
                     op0=OP.mult, op1=OP.mult), inc=s_dve)
            yev_ord[grp] = s_dve.n
            head_free[h] = pv_ord[gb]
            if j0 == 12 and h + 2 < H:
                head_loads(h + 2)

        for i in range(NB + LAG):
            if i < NB:
                emit_S(i)
            if 0 <= i - LAG < NB:
                emit_lpv(i - LAG)
        p.run()
        wo_ready = wo_cast[15]
    barrier("attn")
    if stop == "attn":
        return _fin()

    p = Phase(nc, "p5")
    with contextlib.ExitStack() as ph:
        def psb(name, shape, dtype):
            return ph.enter_context(nc.sbuf_tensor(name, shape, dtype))

        def pps(name, shape, dtype=F32):
            return ph.enter_context(nc.psum_tensor(name, shape, dtype))


        xo = [psb(f"xo{i}", [128, D], F32) for i in range(2)]
        a_sb = [psb(f"asb{i}", [128, D], F32) for i in range(2)]
        x1_sb = [psb(f"x1sb{i}", [128, D], F32) for i in range(2)]
        xn3 = [psb(f"xn3_{i}", [128, D], BF16) for i in range(2)]
        st24a = psb("p5sta", [128, 24], F32)
        st24b = psb("p5stb", [128, 24], F32)
        mv6a = psb("p5mva", [128, 2], F32)
        mv6b = psb("p5mvb", [128, 2], F32)
        vrow = [psb(f"p5v{i}", [128, 1], F32) for i in range(2)]
        rrow = [psb(f"p5r{i}", [128, 1], F32) for i in range(2)]
        ps_o = pps("ps_o", [128, D])

        o_ev, o_st, o_mm, o_add, o_ap3 = {}, {}, {}, {}, {}

        def ln_bn(src, par, st24, mv6, rv, wfirst):
            for c4 in range(4):
                p.em("vector", wfirst if c4 == 0 else [],
                     lambda e, par=par, src=src, st24=st24, c4=c4: e.bn_stats(
                         out=st24[:, c4 * 6:(c4 + 1) * 6],
                         in_=src[par][:, c4 * 512:(c4 + 1) * 512]),
                     inc=s_dve if c4 == 3 else None)
            p.em("vector", [(s_dve, s_dve.n)], lambda e, st24=st24, mv6=mv6:
                 e.bn_aggr(out=mv6[:], in_=st24[:]), inc=s_dve)
            agg = s_dve.n
            p.em("gpsimd", [(s_dve, agg)], lambda e, mv6=mv6, rv=rv:
                 e.tensor_scalar(
                     out=vrow[rv][:], in0=mv6[:, 1:2], scalar1=1.0,
                     scalar2=EPS, op0=OP.mult, op1=OP.add), inc=s_pool)
            sdone = s_pool.n
            p.em("scalar", [(s_pool, sdone)], lambda e, rv=rv: e.activation(
                out=vrow[rv][:], in_=vrow[rv][:], func=AF.Ln))
            p.em("scalar", [], lambda e, rv=rv: e.activation(
                out=rrow[rv][:], in_=vrow[rv][:], func=AF.Exp, scale=-0.5),
                inc=s_act)
            return s_act.n

        for tc in range(16):
            par = tc % 2
            w = [(s_act, wo_ready)] if tc < 2 else [(s_dve, o_add[tc - 2])]
            p.em("sync", w, lambda e, tc=tc, par=par: e.dma_start(
                out=xo[par][:], in_=x_own[tc * 128:(tc + 1) * 128, :]),
                inc=s_dx, k=16)
            xld = s_dx.n
            w = [(s_act, wo_ready)]
            if tc >= 1:
                w.append((s_act, o_ev[tc - 1]))

            def omms(e, tc=tc):
                last = None
                for oc in range(4):
                    for hj in range(16):
                        last = e.matmul(ps_o[:, oc * 512:(oc + 1) * 512],
                                        y_sb[:, hj, tc * 128:(tc + 1) * 128],
                                        wo_bf[:, hj, oc * 512:(oc + 1) * 512],
                                        start=(hj == 0), stop=(hj == 15))
                return last
            p.em("tensor", w, omms, inc=s_pe)
            o_mm[tc] = s_pe.n
            p.em("scalar", [(s_pe, o_mm[tc])], lambda e, par=par: e.copy(
                out=a_sb[par][:], in_=ps_o[:]), inc=s_act)
            o_ev[tc] = s_act.n

            rd2 = ln_bn(a_sb, par, st24a, mv6a, 0, [(s_act, o_ev[tc])])
            p.em("vector", [(s_dx, xld), (s_act, rd2)], lambda e, par=par:
                 e.tensor_scalar(
                     out=a_sb[par][:], in0=a_sb[par][:], scalar1=mv6a[:, 0:1],
                     scalar2=rrow[0][:], op0=OP.subtract, op1=OP.mult))
            w = [] if tc < 2 else [(s_do, o_st[tc - 2])]
            p.em("vector", w, lambda e, par=par: e.tensor_tensor(
                out=x1_sb[par][:], in0=a_sb[par][:], in1=xo[par][:], op=OP.add),
                inc=s_dve)
            o_add[tc] = s_dve.n
            rd3 = ln_bn(x1_sb, par, st24b, mv6b, 1, [])
            p.em("vector", [(s_act, rd3)], lambda e, par=par: e.tensor_scalar(
                out=xn3[par][:], in0=x1_sb[par][:], scalar1=mv6b[:, 0:1],
                scalar2=rrow[1][:], op0=OP.subtract, op1=OP.mult), inc=s_dve)
            o_ap3[tc] = s_dve.n
            p.em("gpsimd", [(s_dve, o_ap3[tc])], lambda e, tc=tc, par=par:
                 e.dma_start(out=xn3_scr[tc * 128:(tc + 1) * 128, :],
                             in_=xn3[par][:]), inc=s_do, k=16)
            p.em("gpsimd", [], lambda e, tc=tc, par=par: e.dma_start(
                out=x1_scr[tc * 128:(tc + 1) * 128, :], in_=x1_sb[par][:]),
                inc=s_do, k=16)
            o_st[tc] = s_do.n
        p.run()
    attn_stack.close()
    barrier("p5")
    if stop == "p5":
        return _fin()

    p = Phase(nc, "p6")
    with contextlib.ExitStack() as ph:
        def psb(name, shape, dtype):
            return ph.enter_context(nc.sbuf_tensor(name, shape, dtype))

        def pps(name, shape, dtype=F32):
            return ph.enter_context(nc.psum_tensor(name, shape, dtype))

        sgn_bf = psb("sgnbf", [128, NI, D], BF16)
        alphab = psb("alphab", [128, D], F32)
        biasb = psb("biasb", [128, D], F32)
        pre_stack = contextlib.ExitStack()

        def presb(name, shape, dtype):
            return pre_stack.enter_context(nc.sbuf_tensor(name, shape, dtype))

        wf_f = [presb(f"wff{i}", [128, D], F32) for i in range(4)]
        absw = [presb(f"absw{i}", [128, D], BF16) for i in range(4)]
        arow = presb("arow", [1, D], F32)
        ahi = presb("ahi", [1, D], BF16)
        ahif = presb("ahif", [1, D], F32)
        alo = presb("alo", [1, D], BF16)
        brow = presb("brow", [1, D], F32)
        bhi = presb("bhi", [1, D], BF16)
        bhif = presb("bhif", [1, D], F32)
        blo = presb("blo", [1, D], BF16)
        ps_al = [pps(f"psal{i}", [1, 512]) for i in range(4)]
        ps_ab = pps("ps_ab", [128, 512])
        ps_f = [pps(f"psf{i}", [128, 512]) for i in range(3)]

        sgn_act, al_mm = {}, {}
        for i in range(NI):
            w = [] if i < 4 else [(s_act, sgn_act[i - 4])]
            p.em("sync", w, lambda e, i=i: e.dma_start(
                out=wf_f[i % 4][:], in_=wfT[i * 128:(i + 1) * 128, :]),
                inc=s_dx, k=16)
            ld = s_dx.n
            p.em("scalar", [(s_dx, ld)], lambda e, i=i: e.sign(
                out=sgn_bf[:, i, :], in_=wf_f[i % 4][:]))
            w = [] if i < 4 else [(s_pe, al_mm[i - 4])]
            p.em("scalar", w, lambda e, i=i: e.activation(
                out=absw[i % 4][:], in_=wf_f[i % 4][:], func=AF.Abs), inc=s_act)
            sgn_act[i] = s_act.n
            p.em("tensor", [(s_act, sgn_act[i])], lambda e, i=i: [
                e.matmul(ps_al[oc][:], ones_col[:],
                         absw[i % 4][:, oc * 512:(oc + 1) * 512],
                         start=(i == 0), stop=(i == NI - 1)) for oc in range(4)][-1],
                inc=s_pe)
            al_mm[i] = s_pe.n
        alpha_mm_done = s_pe.n
        p.em("sync", [], lambda e: e.dma_start(out=brow[:], in_=fbias),
             inc=s_dx, k=16)
        brow_ld = s_dx.n
        for oc in range(4):
            p.em("vector", [(s_pe, alpha_mm_done)], lambda e, oc=oc: e.tensor_scalar(
                out=arow[0:1, oc * 512:(oc + 1) * 512], in0=ps_al[oc][:],
                scalar1=1.0 / D, scalar2=None, op0=OP.mult),
                inc=s_dve if oc == 3 else None)
        p.em("vector", [(s_dx, brow_ld), (s_dve, s_dve.n)], lambda e: e.nop(),
             inc=s_dve)
        rows_done = s_dve.n
        for srow, dst in ((arow, alphab), (brow, biasb)):
            for q in (0, 32, 64, 96):
                p.em("vector", [(s_dve, rows_done)],
                     lambda e, srow=srow, dst=dst, q=q: e.tensor_copy(
                         out=dst[q:q + 1, :], in_=srow[:]),
                     inc=s_dve if q == 96 else None)
            p.em("vector", [(s_dve, s_dve.n)], lambda e, dst=dst:
                 e.stream_shuffle(out=dst[:], in_=dst[:], mask=[0] * 32),
                 inc=s_dve)
        p.em("vector", [(s_dve, s_dve.n)], lambda e: e.nop(), inc=s_act,
             k=1)
        bcast_done = s_act.n
        if "p6dbg" in taps:
            p.em("gpsimd", [(s_act, bcast_done)], lambda e: e.dma_start(
                out=dbg_alpha, in_=alphab[:]), inc=s_do, k=16)
            p.em("gpsimd", [], lambda e: e.dma_start(
                out=dbg_bias, in_=biasb[:]), inc=s_do, k=16)
            p.em("gpsimd", [], lambda e: e.dma_start(
                out=dbg_sgn, in_=sgn_bf[:]), inc=s_do, k=16)
        if stop == "p6pre":
            p.em("sync", [(s_do, s_do.n)], lambda e: None)
            p.run()
            pre_stack.close()
            ph.close()
            est.close()
            return nc
        pre_stack.close()

        x3t = [psb(f"x3t{i}", [128, NI, 128], BF16) for i in range(2)]
        x1t = [psb(f"x1t{i}", [128, D], F32) for i in range(2)]
        f_sb = [psb(f"fsb{i}", [128, D], F32) for i in range(2)]
        o_sb = [psb(f"osb{i}", [128, D], F32) for i in range(2)]
        dum6 = psb("dum6", [128, D], BF16)
        st_s1 = psb("p6s1", [128, 1], F32)
        st_s2 = psb("p6s2", [128, 1], F32)
        st24 = [psb(f"p6st{i}", [128, 24], F32) for i in range(2)]
        mv6 = [psb(f"p6mv{i}", [128, 2], F32) for i in range(2)]
        mrow = [psb(f"p6m{i}", [128, 1], F32) for i in range(2)]
        vrow = [psb(f"p6v{i}", [128, 1], F32) for i in range(2)]
        rrow = [psb(f"p6r{i}", [128, 1], F32) for i in range(2)]
        nmr = [psb(f"p6n{i}", [128, 1], F32) for i in range(2)]

        f_ev, f_st, add_ord, sdn, rdn, bnst_ord = {}, {}, {}, {}, {}, {}

        def ln4_b(tj):
            par = tj % 2
            p.em("vector", [(s_dve, bnst_ord[tj])], lambda e, par=par:
                 e.bn_aggr(out=mv6[par][:], in_=st24[par][:]), inc=s_dve)
            agg = s_dve.n
            p.em("gpsimd", [(s_dve, agg)], lambda e, par=par: e.tensor_scalar(
                out=vrow[par][:], in0=mv6[par][:, 1:2], scalar1=1.0,
                scalar2=EPS, op0=OP.mult, op1=OP.add), inc=s_pool)
            sdn[tj] = s_pool.n
            p.em("scalar", [(s_pool, sdn[tj])], lambda e, par=par: e.activation(
                out=vrow[par][:], in_=vrow[par][:], func=AF.Ln))
            p.em("scalar", [], lambda e, par=par: e.activation(
                out=rrow[par][:], in_=vrow[par][:], func=AF.Exp, scale=-0.5),
                inc=s_act)
            rdn[tj] = s_act.n

        def ln4_c(tj):
            par = tj % 2
            wex = [(s_act, rdn[tj])]
            if "p6dbg" in taps and tj == 0:
                p.em("gpsimd", [(s_dve, f_ev[("d", 0, 3)])], lambda e: e.dma_start(
                    out=dbg_f, in_=f_sb[0][:]), inc=s_do, k=16)
                wex = [(s_do, s_do.n)]
            p.em("vector", wex, lambda e, par=par: e.tensor_scalar(
                out=f_sb[par][:], in0=f_sb[par][:], scalar1=mv6[par][:, 0:1],
                scalar2=rrow[par][:], op0=OP.subtract, op1=OP.mult))
            w = [] if tj < 2 else [(s_do, f_st[tj - 2])]
            p.em("vector", w, lambda e, par=par: e.tensor_tensor(
                out=o_sb[par][:], in0=f_sb[par][:], in1=x1t[par][:], op=OP.add),
                inc=s_dve)
            odone = s_dve.n
            add_ord[tj] = odone
            p.em("gpsimd", [(s_dve, odone)], lambda e, tj=tj, par=par:
                 e.dma_start(out=out_own[tj * 128:(tj + 1) * 128, :],
                             in_=o_sb[par][:]), inc=s_do, k=16)
            f_st[tj] = s_do.n

        for tc in range(16):
            par = tc % 2
            w = [(s_act, bcast_done)] if tc < 2 else [(s_pe, f_ev[tc - 2])]
            p.em("sync", w, lambda e, tc=tc, par=par: e.dma_start_transpose(
                out=x3t[par][:], in_=xn3_scr[tc * 128:(tc + 1) * 128, :]),
                inc=s_dx, k=16)
            x3ld = s_dx.n
            w = [(s_act, bcast_done)] if tc < 2 else [(s_dve, add_ord[tc - 2])]
            p.em("gpsimd", w, lambda e, tc=tc, par=par: e.dma_start(
                out=x1t[par][:], in_=x1_scr[tc * 128:(tc + 1) * 128, :]),
                inc=s_do, k=16)
            x1ld = s_do.n
            for oc in range(4):
                opar = (tc * 4 + oc) % 3
                w = [(s_dx, x3ld), (s_do, x1ld)]
                if tc == 0 and oc == 0:
                    w.append((s_act, bcast_done))
                if (tc * 4 + oc) >= 3:
                    pli = tc * 4 + oc - 3
                    w.append((s_dve, f_ev[("d", pli // 4, pli % 4)]))

                def fmms(e, tc=tc, par=par, opar=opar, oc=oc):
                    last = None
                    for i in range(NI):
                        last = e.matmul(ps_f[opar][:], x3t[par][:, i, :],
                                        sgn_bf[:, i, oc * 512:(oc + 1) * 512],
                                        start=(i == 0), stop=(i == NI - 1))
                    return last
                p.em("tensor", w, fmms, inc=s_pe)
                fm = s_pe.n
                if oc == 3:
                    f_ev[tc] = fm
                p.em("vector", [(s_pe, fm)], lambda e, par=par, opar=opar, oc=oc:
                     e.tensor_tensor(out=f_sb[par][:, oc * 512:(oc + 1) * 512],
                                     in0=ps_f[opar][:],
                                     in1=alphab[:, oc * 512:(oc + 1) * 512],
                                     op=OP.mult), inc=s_dve)
                f_ev[("d", tc, oc)] = s_dve.n
                p.em("gpsimd", [(s_dve, f_ev[("d", tc, oc)])],
                     lambda e, par=par, oc=oc: e.tensor_tensor(
                         out=f_sb[par][:, oc * 512:(oc + 1) * 512],
                         in0=f_sb[par][:, oc * 512:(oc + 1) * 512],
                         in1=biasb[:, oc * 512:(oc + 1) * 512], op=OP.add),
                     inc=s_pool)
                f_ev[("b", tc, oc)] = s_pool.n
                p.em("vector", [(s_pool, f_ev[("b", tc, oc)])],
                     lambda e, par=par, oc=oc: e.bn_stats(
                         out=st24[par][:, oc * 6:(oc + 1) * 6],
                         in_=f_sb[par][:, oc * 512:(oc + 1) * 512]),
                     inc=s_dve if oc == 3 else None)
                if oc == 3:
                    bnst_ord[tc] = s_dve.n
                if tc >= 1:
                    if oc == 0:
                        ln4_b(tc - 1)
                    elif oc == 1:
                        ln4_c(tc - 1)
        ln4_b(15)
        ln4_c(15)
        p.em("sync", [(s_do, s_do.n)], lambda e: None)
        p.run()
    est.close()
    return nc


_NC_CACHE = {}


def _get_nc():
    if "nc" not in _NC_CACHE:
        _NC_CACHE["nc"] = build_nc()
    return _NC_CACHE["nc"]


def _rope_tables():
    af = (1.0 / 1024.0) ** np.linspace(0.0, 1.0, 32, dtype=np.float32)
    t = np.arange(T, dtype=np.float32)
    theta = t[:, None] * af[None, :]
    cos = np.cos(theta).T.astype(np.float32)
    sin = np.sin(theta).T.astype(np.float32)
    C = np.ones((128, T), np.float32)
    S = np.zeros((128, T), np.float32)
    C[0:32] = cos
    C[64:96] = cos
    S[0:32] = sin
    S[64:96] = -sin
    return C, S


def kernel(L0_attn_x, L0_attn_Wq, L0_attn_Wk, L0_attn_Wv, L0_attn_Wo,
           L0_ffn_weight, L0_ffn_bias):
    x = np.ascontiguousarray(np.asarray(L0_attn_x, np.float32))
    wq = np.asarray(L0_attn_Wq, np.float32)
    wk = np.asarray(L0_attn_Wk, np.float32)
    wv = np.asarray(L0_attn_Wv, np.float32)
    wo = np.asarray(L0_attn_Wo, np.float32)
    wf = np.asarray(L0_ffn_weight, np.float32)
    fb = np.asarray(L0_ffn_bias, np.float32).reshape(1, D)

    wqT = np.ascontiguousarray(
        wq.T.reshape(NI, 128, H, 128).transpose(2, 1, 0, 3))
    wkT = np.ascontiguousarray(
        wk.T.reshape(NI, 128, H, 128).transpose(2, 1, 0, 3))
    wvT = np.ascontiguousarray(wv.T)
    woT = np.ascontiguousarray(wo.T)
    wfT = np.ascontiguousarray(wf.T)

    C, S = _rope_tables()
    diag = np.triu(np.ones((128, 128), np.float32)).astype(ml_dtypes.bfloat16)
    ones_m = np.ones((128, 128), np.float32).astype(ml_dtypes.bfloat16)
    zeros_m = np.zeros((128, 128), np.float32).astype(ml_dtypes.bfloat16)
    ident = np.eye(128, dtype=np.float32).astype(ml_dtypes.bfloat16)

    nc = _get_nc()
    in_maps = []
    for core in range(8):
        b, hh = core // 2, core % 2
        xt = x[b].reshape(32, 128, D)
        own_tiles = list(range(hh, 32, 2))
        x_own_c = np.ascontiguousarray(xt[own_tiles].reshape(T // 2, D))
        own_cols = np.concatenate(
            [np.arange(t * 128, (t + 1) * 128) for t in own_tiles])
        in_maps.append({
            "x_full": x[b],
            "x_own": x_own_c,
            "wqT": wqT, "wkT": wkT, "wvT": wvT, "woT": woT, "wfT": wfT,
            "fbias": fb,
            "ropeC": C, "ropeS": S,
            "ropeCq": np.ascontiguousarray(C[:, own_cols]),
            "ropeSq": np.ascontiguousarray(S[:, own_cols]),
            "maskP_in": diag if hh == 0 else ones_m,
            "maskQ_in": zeros_m if hh == 0 else diag,
            "ident_in": ident,
        })
    res = run_bass_kernel_spmd(nc, in_maps, core_ids=list(range(8)))
    out = np.empty((4, T, D), np.float32)
    for core in range(8):
        b, hh = core // 2, core % 2
        shard = np.asarray(res.results[core]["out_own"]).reshape(16, 128, D)
        out[b].reshape(32, 128, D)[list(range(hh, 32, 2))] = shard
    return out
